# version 12
# baseline (speedup 1.0000x reference)
import os
import numpy as np
import concourse.bass as bass
import concourse.mybir as mybir
from concourse.bass_utils import run_bass_kernel_spmd

F32 = mybir.dt.float32
F32R = mybir.dt.float32r
BF16 = mybir.dt.bfloat16
AF = mybir.ActivationFunctionType
ALU = mybir.AluOpType

D = 1024
DI = 2048
L = 4096
NCH = 8
T = 512
HF = 256
KD = 8
CD = 16
NST = 16
DTR = 64
HID = 4096
KH = 32
EPS = 1e-5


class Sched:
    def __init__(self):
        self.steps = []
        self.total = {}
        self.sem_names = set()
        self.small_toks = set()

    def step(self, engine, fn, waits=(), sem=None, inc=None, tag="",
             small=False, strict=False):
        if sem is None:
            sem = {"sync": "DW", "tensor": "T", "vector": "V",
                   "scalar": "S", "gpsimd": "G"}[engine]
        if inc is None:
            inc = 16 if engine == "sync" else 1
        for w in waits:
            assert w is not None, f"None wait in {tag}"
            (s, v) = w
            assert s in self.total and v <= self.total[s], \
                f"wait on future token {s}={v} (now {self.total.get(s)}) tag={tag}"
        self.total[sem] = self.total.get(sem, 0) + inc
        self.sem_names.add(sem)
        self.steps.append((engine, fn, sem, inc, tuple(waits), tag, strict))
        tok = (sem, self.total[sem])
        if small:
            self.small_toks.add(tok)
        return tok

    def emit(self, nc, block):
        sems = {name: nc.alloc_semaphore("sem_" + name)
                for name in sorted(self.sem_names)}
        own = {"sync": None, "tensor": "T", "vector": "V", "scalar": "S",
               "gpsimd": "G"}

        def run(eng_name, eng):
            waited = {}
            for (engine, fn, sem, inc, waits, tag, strict) in self.steps:
                if engine != eng_name:
                    continue
                for (s, v) in waits:
                    if (s == own[eng_name] and not STRICT and not strict
                            and (s, v) not in self.small_toks):
                        continue
                    if waited.get(s, 0) >= v:
                        continue
                    eng.wait_ge(sems[s], v)
                    waited[s] = v
                last = fn(eng)
                last.then_inc(sems[sem], inc)

        @block.sync
        def _(eng):
            run("sync", eng)

        @block.tensor
        def _(eng):
            run("tensor", eng)

        @block.vector
        def _(eng):
            run("vector", eng)

        @block.scalar
        def _(eng):
            run("scalar", eng)

        if any(s[0] == "gpsimd" for s in self.steps):
            @block.gpsimd
            def _(eng):
                run("gpsimd", eng)


class Ring:
    def __init__(self, aps):
        self.aps = list(aps)
        self.free_tok = [None] * len(self.aps)
        self.i = 0

    def acquire(self):
        k = self.i % len(self.aps)
        self.i += 1
        tok = self.free_tok[k]
        return k, self.aps[k], ([tok] if tok else [])

    def release(self, k, token):
        self.free_tok[k] = token


STRICT = os.environ.get("MKRC", "0") == "1"


def build(nchunks=NCH):
    nc = bass.Bass("TRN2", target_bir_lowering=False,
                   detect_race_conditions=STRICT)

    def din(name, shape, dtype=F32):
        return nc.dram_tensor(name, shape, dtype, kind="ExternalInput").ap()

    x_hbm = din("x", [L, D])
    w_in_t = din("w_in_t", [D, 2 * DI])
    w_out_t = din("w_out_t", [DI, D])
    w_xp_fm = din("w_xp_fm", [128, CD * 96])
    w_dt_t = din("w_dt_t", [DTR, DI])
    w1_t = din("w1_t", [D, HID])
    w2_t = din("w2_t", [D, HID])
    h1_t = din("h1_t", [HID, HID])
    h2_t = din("h2_t", [HID, HID])
    w3_t = din("w3_t", [HID, D])
    conv_fm = din("conv_fm", [128, CD * 4])
    convb_fm = din("convb_fm", [128, CD])
    dtb_fm = din("dtb_fm", [128, CD])
    dsk_fm = din("dsk_fm", [128, CD])
    alog_fm = din("alog_fm", [128, CD * NST])
    norm_fm = din("norm_fm", [128, KD])
    s1_fm = din("s1_fm", [128, KH])
    s2_fm = din("s2_fm", [128, KH])
    h1s_fm = din("h1s_fm", [128, KH])
    h2s_fm = din("h2s_fm", [128, KH])
    w3s_row = din("w3s_row", [1, D])
    ident_in = din("ident_in", [128, 128])
    ones_in = din("ones_in", [1, 128])

    out_hbm = nc.dram_tensor("out", [L, D], F32, kind="ExternalOutput").ap()

    scr = {}
    for nm, ntiles, tcols in [
        ("in", 32, 1024), ("out", 16, 1024), ("w1", 32, 1024), ("w2", 32, 1024),
        ("h1", 128, 1024), ("h2", 128, 1024), ("w3", 32, 1024),
    ]:
        scr[nm] = nc.dram_tensor("scr_" + nm, [ntiles * 128, tcols], BF16,
                                 kind="Internal").ap()

    def sb(name, shape, dtype=F32):
        return nc.alloc_sbuf_tensor(name, shape, dtype).ap()

    xr = Ring([sb(f"xr{i}", [128, 1024]) for i in range(1)])
    exr = Ring([sb(f"exr{i}", [128, 512]) for i in range(1)])
    stats = sb("stats", [128, 32])
    xnbf = sb("xnbf", [128, KD * T], BF16)
    xssm = sb("xssm", [128, CD * HF])
    xspre = Ring([sb(f"xspre{i}", [128, HF + 4]) for i in range(2)])
    xsbf = Ring([sb(f"xsbf{i}", [128, HF], BF16) for i in range(1)])
    delta = Ring([sb(f"delta{i}", [128, HF]) for i in range(2)])
    uten = Ring([sb("uten0", [128, HF])])
    dAr = Ring([sb(f"dA{i}", [128, HF]) for i in range(2)])
    dBr = Ring([sb(f"dB{i}", [128, HF]) for i in range(1)])
    srng = Ring([sb(f"sr{i}", [128, HF]) for i in range(1)])
    pr = Ring([sb(f"pr{i}", [128, HF]) for i in range(4)])
    ybuf = Ring([sb(f"ybuf{i}", [128, HF]) for i in range(2)])
    bbt = [sb(f"bb{n}", [128, HF]) for n in range(NST)]
    cbt = [sb(f"cb{n}", [128, HF]) for n in range(NST)]
    gate = Ring([sb(f"gate{i}", [128, HF]) for i in range(2)])
    ygt = sb("ygt", [128, CD * T], BF16)
    ygstage = sb("ygstage", [128, CD * HF], BF16)
    ffnA = sb("ffnA", [128, KH * T], BF16)
    ffnB = sb("ffnB", [128, KH * T], BF16)
    scr1 = Ring([sb(f"scr1_{i}", [128, 512]) for i in range(2)])
    fring = Ring([sb(f"fr{i}", [128, 512]) for i in range(1)])
    hstat = Ring([sb(f"hstat{i}", [128, 24]) for i in range(2)])
    outr = Ring([sb(f"outr{i}", [128, 512]) for i in range(1)])
    wring = Ring([sb(f"wr{i}", [128, 1024], BF16) for i in range(6)])
    state = sb("state", [128, CD * NST])
    xpbf = sb("xpbf", [128, CD * 96], BF16)
    dtbf = sb("dtbf", [DTR, DI], BF16)
    deltar = sb("deltar", [DTR, HF], BF16)
    bcsb = sb("bcsb", [32, HF], F32R)
    ares = sb("ares", [128, CD * NST])
    convw = sb("convw", [128, CD * 4])
    convb = sb("convb", [128, CD])
    dtb = sb("dtb", [128, CD])
    dsk = sb("dsk", [128, CD])
    normw = sb("normw", [128, KD])
    s1c = sb("s1c", [128, KH])
    s2c = sb("s2c", [128, KH])
    h1sc = sb("h1sc", [128, KH])
    h2sc = sb("h2sc", [128, KH])
    w3sb = sb("w3sb", [128, D])
    identf = sb("identf", [128, 128])
    identr = sb("identr", [128, 128], F32R)
    onesr = sb("onesr", [1, 128], F32R)
    rowt = Ring([sb(f"rowt{i}", [1, HF], F32R) for i in range(1)])
    rr2b = sb("rr2b", [128, HF])
    lnscr = sb("lnscr", [128, 128])
    tails = sb("tails", [128, CD * 3])

    def psalloc(name):
        return nc.alloc_psum_tensor(name, [128, 512], F32).ap()

    pm = [psalloc("pm0"), psalloc("pm1")]
    pf = [psalloc(f"pf{i}") for i in range(4)]
    py = [psalloc("py0"), psalloc("py1")]

    sch = Sched()

    def dma(dst, src, waits=(), tag="", sem="DW"):
        return sch.step("sync", lambda e: e.dma_start(out=dst, in_=src),
                        waits=waits, sem=sem, tag=tag)

    for dst, src_ in [
        (convw, conv_fm), (convb, convb_fm), (dtb, dtb_fm), (dsk, dsk_fm),
        (ares, alog_fm), (normw, norm_fm), (s1c, s1_fm), (s2c, s2_fm),
        (h1sc, h1s_fm), (identf, ident_in), (h2sc, h2s_fm),
        (w3sb[0:1, :], w3s_row),
    ]:
        dma(dst, src_, tag="p:load", sem="DW")
    ptok_all = ("DW", sch.total["DW"])
    ptoks = [ptok_all]
    fk0, fs0, fw0 = fring.acquire()
    t_of = dma(fs0[0:1, 0:128], ones_in, waits=fw0, sem="FR0", tag="p:onesf")
    t_onr2 = sch.step("scalar", lambda e, s=fs0: e.copy(out=onesr,
                                                        in_=s[0:1, 0:128]),
                      waits=[t_of], tag="p:onr2")
    fring.release(fk0, t_onr2)
    t_a1 = sch.step("scalar", lambda e: e.activation(out=ares, in_=ares, func=AF.Exp),
                    waits=ptoks, tag="p:expA")
    t_a2 = sch.step("scalar", lambda e: e.activation(out=ares, in_=ares,
                                                     func=AF.Identity, scale=-1.0),
                    waits=[t_a1], tag="p:negA")
    t_idr = sch.step("scalar", lambda e: e.copy(out=identr, in_=identf),
                     waits=ptoks, tag="p:idr")
    t_onr = t_onr2
    w3sb_tok = None
    last_pm_seed = [None, None]
    for j in range(4):
        rk, rs, rw = rowt.acquire()
        tc = sch.step("scalar", lambda e, r=rs, j=j: e.copy(
            out=r, in_=w3sb[0:1, j * 256:(j + 1) * 256]),
            waits=ptoks + rw + ([w3sb_tok] if w3sb_tok else []), tag=f"p:w3r{j}")
        tm = sch.step("tensor", lambda e, r=rs, j=j: e.matmul(
            out=pm[j % 2][:, 0:256], lhsT=onesr, rhs=r, start=True, stop=True,
            skip_group_check=True), waits=[tc, t_onr], tag=f"p:w3m{j}")
        rowt.release(rk, tm)
        w3sb_tok = sch.step("scalar", lambda e, j=j: e.copy(
            out=w3sb[:, j * 256:(j + 1) * 256], in_=pm[j % 2][:, 0:256]),
            waits=[tm], tag=f"p:w3e{j}")
        last_pm_seed[j % 2] = w3sb_tok

    xp_toks = []
    for j in range(3):
        fk, fs, fw = fring.acquire()
        dtok = dma(fs[:, 0:512], w_xp_fm[:, j * 512:(j + 1) * 512],
                   waits=fw + [ptok_all], sem=f"FR{fk}", tag=f"p:xpf{j}")
        ctok = sch.step("scalar", lambda e, s=fs, jj=j: e.copy(
            out=xpbf[:, jj * 512:(jj + 1) * 512], in_=s[:, 0:512]),
            waits=[dtok], tag=f"p:xpc{j}")
        fring.release(fk, ctok)
        xp_toks.append(ctok)
    for j in range(4):
        fk, fs, fw = fring.acquire()
        dtok = dma(fs[0:DTR, 0:512], w_dt_t[:, j * 512:(j + 1) * 512],
                   waits=fw, sem=f"FR{fk}", tag=f"p:dtf{j}")
        ctok = sch.step("scalar", lambda e, s=fs, jj=j: e.copy(
            out=dtbf[:, jj * 512:(jj + 1) * 512], in_=s[0:DTR, 0:512]),
            waits=[dtok], tag=f"p:dtc{j}")
        fring.release(fk, ctok)
        xp_toks.append(ctok)
    t_z1 = sch.step("vector", lambda e: e.memset(state, 0.0), tag="p:zst")
    t_z2 = sch.step("vector", lambda e: e.memset(tails, 0.0), tag="p:ztl")
    xpc_tok = xp_toks[-1]

    class WMat:

        def __init__(self, name, src, scratch, scale_cols=None, w3_scale=False):
            self.name = name
            self.src = src
            self.scratch = scratch
            self.scale_cols = scale_cols
            self.w3_scale = w3_scale
            self.seen = {}
            self.next_slot = 0

        def convert(self, k0, nk, o0, no):
            key = (k0, nk, o0, no)
            assert key not in self.seen
            ti = self.next_slot
            self.next_slot += 1
            bk, bslot, bw = cvb.acquire()
            cast_toks = []
            nb = max(1, 512 // no)
            kb = 0
            while kb < nk:
                g = min(nb, nk - kb)
                src3 = self.src[(k0 + kb) * 128:(k0 + kb + g) * 128, o0:o0 + no]
                fk, fs, fw = cvf.acquire()
                if g > 1:
                    src_ap = src3.rearrange("(b p) o -> p b o", p=128)
                    dst_ap = fs.rearrange("p (b o) -> p b o", b=g)
                else:
                    src_ap = src3
                    dst_ap = fs[:, 0:no]
                dtok = sch.step("sync", lambda e, d=dst_ap, a=src_ap:
                                e.dma_start(out=d, in_=a),
                                waits=fw + bw, sem=f"CF{fk}",
                                tag=f"{self.name}cf{key}.{kb}")
                bw = []
                tgt = bslot[:, kb * no:(kb + g) * no]
                if self.scale_cols is not None:
                    ct = dtok
                    for b2 in range(g):
                        kk = k0 + kb + b2
                        ct = sch.step("gpsimd", lambda e, s=fs, b2=b2, kk=kk,
                                      kb=kb, no=no, bs=bslot:
                                      e.tensor_scalar(
                            out=bs[:, (kb + b2) * no:(kb + b2 + 1) * no],
                            in0=s[:, b2 * no:(b2 + 1) * no],
                            scalar1=self.scale_cols[:, kk:kk + 1], scalar2=0.0,
                            op0=ALU.mult, op1=ALU.add),
                            waits=[dtok], tag=f"{self.name}cv{key}.{kb}.{b2}")
                    ctok = ct
                elif self.w3_scale:
                    ctok = sch.step("gpsimd", lambda e, s=fs, t=tgt, g=g, no=no:
                                    e.tensor_tensor(
                        out=t, in0=s[:, 0:g * no],
                        in1=w3sb[:, o0:o0 + no], op=ALU.mult),
                        waits=[dtok, w3sb_tok], tag=f"{self.name}cv{key}.{kb}")
                else:
                    ctok = sch.step("gpsimd", lambda e, s=fs, t=tgt, g=g, no=no:
                                    e.tensor_copy(out=t, in_=s[:, 0:g * no]),
                                    waits=[dtok], tag=f"{self.name}cv{key}.{kb}")
                cvf.release(fk, ctok)
                cast_toks.append(ctok)
                kb += g
            wbt = sch.step("sync", lambda e, s=bslot, nn=nk * no, t=ti:
                           e.dma_start(out=self.scratch[t * 128:(t + 1) * 128,
                                                        0:nn],
                                       in_=s[:, 0:nn]),
                           waits=cast_toks, sem=f"CB{bk}",
                           tag=f"{self.name}cwb{key}")
            cvb.release(bk, wbt)
            self.seen[key] = (ti, wbt)

        def load(self, k0, nk, o0, no, waits=()):
            key = (k0, nk, o0, no)
            kslot, slot, aw = wring.acquire()
            ti, wbt = self.seen[key]
            src = self.scratch[ti * 128:(ti + 1) * 128, 0:nk * no]
            tok = sch.step("sync", lambda e, s=slot, a=src, nn=nk * no:
                           e.dma_start(out=s[:, 0:nn], in_=a),
                           waits=list(waits) + aw + [wbt],
                           sem=f"WR{kslot}", tag=f"{self.name}ld{key}")
            return kslot, slot, tok

    mats = {
        "in": WMat("in", w_in_t, scr["in"], scale_cols=normw),
        "out": WMat("out", w_out_t, scr["out"]),
        "w1": WMat("w1", w1_t, scr["w1"]),
        "w2": WMat("w2", w2_t, scr["w2"]),
        "h1": WMat("h1", h1_t, scr["h1"]),
        "h2": WMat("h2", h2_t, scr["h2"]),
        "w3": WMat("w3", w3_t, scr["w3"], w3_scale=True),
    }

    ffnA_f32 = ffnA.bitcast(F32)
    ffnB_b16 = ffnB
    cvf = Ring([ffnA_f32[:, i * 512:(i + 1) * 512] for i in range(8)])
    cvb = Ring([ffnB_b16[:, i * 1024:(i + 1) * 1024] for i in range(8)])
    for og in range(16):
        mats["in"].convert(0, 4, og * 256, 256)
        mats["in"].convert(4, 4, og * 256, 256)
    for oc in range(KH):
        mats["w1"].convert(0, KD, oc * 128, 128)
        mats["w2"].convert(0, KD, oc * 128, 128)
    for oc in range(KH):
        for q in range(4):
            mats["h1"].convert(q * 8, 8, oc * 128, 128)
    for oc in range(KH):
        for q in range(4):
            mats["h2"].convert(q * 8, 8, oc * 128, 128)
    for oh in range(2):
        for kcg in range(4):
            mats["out"].convert(kcg * 4, 2, oh * 512, 512)
            mats["out"].convert(kcg * 4 + 2, 2, oh * 512, 512)
        for kcg in range(8):
            mats["w3"].convert(kcg * 4, 2, oh * 512, 512)
            mats["w3"].convert(kcg * 4 + 2, 2, oh * 512, 512)
    conv_toks = [t for t in cvf.free_tok + cvb.free_tok if t]
    conv_done = sch.step("gpsimd", lambda e: e.nop(), waits=conv_toks,
                         tag="p:convdone")
    conv_fa = conv_done
    conv_fb = conv_done

    last = {
        "xn_reader": None,
        "ygt_kc": [None] * CD,
        "ffnA_reader": None,
        "ffnB_reader": None,
        "bcb_reader": None,
        "rr2_reader": None,
        "pm_free": [None, None],
        "pf_free": [None, None, None, None],
        "py_free": [None, None],
        "delta_tok": [None] * CD,
        "delta_slot": [None] * CD,
        "xssm_tok": [None] * CD,
        "xssm_reader": [None] * CD,
        "epi": None,
        "ycommit": [None] * CD,
    }
    last["ffnA_reader"] = conv_fa
    last["ffnB_reader"] = conv_fb
    last["pm_free"] = list(last_pm_seed)

    ctx = {}

    def emit_head(k, h):
        par = k % 2
        ctx["rr2_ready"] = {}
        ctx["xn_ready"] = []
        for i in range(2):
            ts = h * 2 + i
            row0 = k * T + ts * 128
            kx, xs_, xw = xr.acquire()
            xtok = sch.step("scalar", lambda e, d=xs_, a=x_hbm[row0:row0 + 128, :]:
                            e.dma_start(out=d, in_=a), waits=xw,
                            sem=f"XR{kx}", inc=16, tag=f"x{k}.{ts}")
            sk, ss, sw = hstat.acquire()
            t1a = sch.step("scalar", lambda e, s=ss, x2=xs_: e.activation(
                out=x2, in_=x2, func=AF.Identity, accum_out=s[:, 18:19]),
                waits=[xtok] + sw, tag=f"sum{k}.{ts}")
            sq_prev = t1a
            for qi in range(8):
                sq_prev = sch.step("scalar", lambda e, s=ss, x2=xs_, qi=qi:
                                   e.activation(
                    out=lnscr, in_=x2[:, qi * 128:(qi + 1) * 128],
                    func=AF.Square, accum_out=s[:, 19 + (qi > 0) * 3:
                                                20 + (qi > 0) * 3]),
                    waits=[sq_prev], tag=f"sq{qi}.{k}.{ts}")
                if qi > 0:
                    sq_prev = sch.step("vector", lambda e, s=ss:
                                       e.tensor_tensor(
                        out=s[:, 19:20], in0=s[:, 19:20], in1=s[:, 22:23],
                        op=ALU.add), waits=[sq_prev],
                        tag=f"sqa{qi}.{k}.{ts}", small=True)
            t1b = sq_prev
            t2a = sch.step("vector", lambda e, s=ss: e.tensor_scalar(
                out=s[:, 8:9], in0=s[:, 18:19], scalar1=1.0 / D, scalar2=0.0,
                op0=ALU.mult, op1=ALU.add), waits=[t1a], tag=f"mn{k}.{ts}", small=True)
            t2b = sch.step("vector", lambda e, s=ss: e.tensor_tensor(
                out=s[:, 20:21], in0=s[:, 8:9], in1=s[:, 8:9], op=ALU.mult),
                waits=[t2a], tag=f"mn2{k}.{ts}", small=True)
            t2c = sch.step("vector", lambda e, s=ss: e.tensor_scalar(
                out=s[:, 21:22], in0=s[:, 19:20], scalar1=1.0 / D, scalar2=0.0,
                op0=ALU.mult, op1=ALU.add), waits=[t1b], tag=f"ms2{k}.{ts}", small=True)
            t2 = sch.step("vector", lambda e, s=ss: e.tensor_tensor(
                out=s[:, 9:10], in0=s[:, 21:22], in1=s[:, 20:21],
                op=ALU.subtract), waits=[t2b, t2c], tag=f"var{k}.{ts}", small=True)
            t3 = sch.step("vector", lambda e, s=ss: e.tensor_scalar(
                out=s[:, 10:11], in0=s[:, 9:10], scalar1=1.0, scalar2=EPS,
                op0=ALU.mult, op1=ALU.add), waits=[t2], tag=f"vpe{k}.{ts}", small=True)
            t4 = sch.step("scalar", lambda e, s=ss: e.activation(
                out=s[:, 11:12], in_=s[:, 10:11], func=AF.Ln), waits=[t3],
                tag=f"ln1{k}.{ts}", small=True)
            t5 = sch.step("scalar", lambda e, s=ss: e.activation(
                out=s[:, 12:13], in_=s[:, 11:12], func=AF.Exp, scale=-0.5),
                waits=[t4], tag=f"rstd{k}.{ts}", small=True)
            t6 = sch.step("vector", lambda e, s=ss: e.tensor_scalar(
                out=s[:, 13:14], in0=s[:, 9:10], scalar1=1.0 + EPS,
                scalar2=EPS * EPS, op0=ALU.mult, op1=ALU.add), waits=[t2],
                tag=f"vp2{k}.{ts}", small=True)
            t7 = sch.step("scalar", lambda e, s=ss: e.activation(
                out=s[:, 14:15], in_=s[:, 13:14], func=AF.Ln), waits=[t6],
                tag=f"ln2{k}.{ts}", small=True)
            t8 = sch.step("vector", lambda e, s=ss: e.tensor_tensor(
                out=s[:, 15:16], in0=s[:, 11:12], in1=s[:, 14:15],
                op=ALU.subtract), waits=[t7, t4], tag=f"dln{k}.{ts}", small=True)
            rr2tok = sch.step("scalar", lambda e, s=ss: e.activation(
                out=s[:, 16:17], in_=s[:, 15:16], func=AF.Exp, scale=0.5),
                waits=[t8], tag=f"rr2{k}.{ts}", small=True)
            st_s1 = stats[:, par * 16 + ts:par * 16 + ts + 1]
            st_s2 = stats[:, par * 16 + 4 + ts:par * 16 + 4 + ts + 1]
            epi_w = [last["epi"]] if last["epi"] else []
            e1 = sch.step("vector", lambda e, s=ss, d=st_s1: e.tensor_scalar(
                out=d, in0=s[:, 12:13], scalar1=1.0, scalar2=1.0,
                op0=ALU.mult, op1=ALU.add), waits=[t5] + epi_w, tag=f"s1{k}.{ts}", small=True)
            e2a = sch.step("vector", lambda e, s=ss: e.tensor_tensor(
                out=s[:, 17:18], in0=s[:, 8:9], in1=s[:, 12:13], op=ALU.mult),
                waits=[t5], tag=f"ms{k}.{ts}", small=True)
            e2 = sch.step("vector", lambda e, s=ss, d=st_s2: e.tensor_scalar(
                out=d, in0=s[:, 17:18], scalar1=-1.0, scalar2=0.0,
                op0=ALU.mult, op1=ALU.add), waits=[e2a] + epi_w, tag=f"s2{k}.{ts}", small=True)
            xntok = sch.step("scalar", lambda e, x2=xs_, s=ss, d=st_s2:
                             e.activation(out=x2, in_=x2, func=AF.Identity,
                                          scale=s[:, 12:13], bias=d),
                             waits=[t5, e2, e1], tag=f"xn{k}.{ts}")
            pmw = [last["pm_free"][i]] if last["pm_free"][i] else []
            trtok = sch.step("tensor", lambda e, s=ss, i=i: e.transpose(
                out=pm[i][0:1, 256:384], in_=s[:, 16:17], identity=identf),
                waits=[rr2tok, t_idr] + pmw, tag=f"rrT{k}.{ts}")
            rk, rs, rw = rowt.acquire()
            retok = sch.step("scalar", lambda e, r=rs, i=i: e.copy(
                out=r[0:1, 0:128], in_=pm[i][0:1, 256:384]),
                waits=[trtok] + rw, tag=f"rrE{k}.{ts}", small=True)
            rmtok = sch.step("tensor", lambda e, r=rs, i=i: e.matmul(
                out=pm[i][:, 256:384], lhsT=onesr, rhs=r[0:1, 0:128],
                start=True, stop=True, skip_group_check=True),
                waits=[retok, t_onr], tag=f"rrM{k}.{ts}")
            rr2w = [last["rr2_reader"]] if last["rr2_reader"] else []
            rbtok = sch.step("scalar", lambda e, i=i: e.copy(
                out=rr2b[:, i * 128:(i + 1) * 128], in_=pm[i][:, 256:384]),
                waits=[rmtok] + rr2w, tag=f"rrB{k}.{ts}")
            last["pm_free"][i] = rbtok
            rowt.release(rk, rmtok)
            ctx["rr2_ready"][ts] = rbtok
            lasttp = None
            for jb in range(2):
                bank = jb
                wv = [xntok, t_idr]
                if last["pm_free"][bank]:
                    wv.append(last["pm_free"][bank])
                def tp_group(e, x2=xs_, b=bank, jb=jb):
                    mm = None
                    for q in range(4):
                        jj = jb * 4 + q
                        mm = e.transpose(
                            out=pm[b][:, q * 128:(q + 1) * 128],
                            in_=x2[:, jj * 128:(jj + 1) * 128], identity=identf)
                    return mm
                ttok = sch.step("tensor", tp_group, waits=wv,
                                tag=f"tp{k}.{ts}.{jb}")
                ttoks = [ttok]
                lasttp = ttok
                etok = None
                for q in range(4):
                    j = jb * 4 + q
                    xnw = [ttoks[-1]]
                    if last["xn_reader"]:
                        xnw.append(last["xn_reader"])
                    etok = sch.step("scalar", lambda e, jj=j, b=bank, qq=q,
                                    t2=ts: e.copy(
                        out=xnbf[:, jj * T + t2 * 128:jj * T + (t2 + 1) * 128],
                        in_=pm[b][:, qq * 128:(qq + 1) * 128]),
                        waits=xnw, tag=f"xe{k}.{ts}.{j}")
                    ctx["xn_ready"].append(etok)
                last["pm_free"][bank] = etok
            hstat.release(sk, rbtok)
            xr.release(kx, lasttp)

    def emit_dt(k, h, c):
        bank = c % 2
        w = [ctx["deltar_tok"], xpc_tok]
        if last["pm_free"][bank]:
            w.append(last["pm_free"][bank])
        mtok = sch.step("tensor", lambda e, c=c, b=bank: e.matmul(
            out=pm[b][:, 0:HF], lhsT=dtbf[:, c * 128:(c + 1) * 128], rhs=deltar,
            start=True, stop=True, skip_group_check=True),
            waits=w, tag=f"dt{k}.{h}.{c}")
        sk, ss, sw = scr1.acquire()
        sp = sch.step("scalar", lambda e, b=bank, s=ss, c=c: e.activation(
            out=s[:, 0:HF], in_=pm[b][:, 0:HF], func=AF.Exp,
            bias=dtb[:, c:c + 1]), waits=[mtok] + sw, tag=f"sp{k}.{h}.{c}")
        dk, dslot, dw = delta.acquire()
        dtok = sch.step("scalar", lambda e, s=ss, d=dslot: e.activation(
            out=d, in_=s[:, 0:HF], func=AF.Ln, bias=1.0),
            waits=[sp] + dw, tag=f"dl{k}.{h}.{c}")
        scr1.release(sk, dtok)
        last["pm_free"][bank] = sp
        last["delta_tok"][c] = dtok
        last["delta_slot"][c] = (dk, dslot)

    def emit_mix(k, h):
        rr_ready = [ctx["rr2_ready"][h * 2], ctx["rr2_ready"][h * 2 + 1]]
        xn_ready = list(ctx["xn_ready"])
        for og in range(8):
            mtok = None
            for q in range(2):
                kslot, slot, wtok = mats["in"].load(q * 4, 4, og * 256, 256)
                bank = og % 2
                mmw = [wtok] + (xn_ready if q == 0 else [])
                if q == 0 and last["pm_free"][bank]:
                    mmw.append(last["pm_free"][bank])

                def in_group(e, s=slot, b=bank, h=h, q=q):
                    mm = None
                    for kb in range(4):
                        for oc2 in range(2):
                            mm = e.matmul(
                                out=pm[b][:, oc2 * HF:(oc2 + 1) * HF],
                                lhsT=s[:, kb * 256 + oc2 * 128:
                                       kb * 256 + (oc2 + 1) * 128],
                                rhs=xnbf[:, (q * 4 + kb) * T + h * HF:
                                         (q * 4 + kb) * T + (h + 1) * HF],
                                start=(q == 0 and kb == 0 and oc2 == 0),
                                stop=(q == 1 and kb == 3),
                                skip_group_check=True)
                    return mm
                mtok = sch.step("tensor", in_group, waits=mmw,
                                tag=f"inmm{k}.{h}.{og}.{q}")
                wring.release(kslot, mtok)
            lastev = None
            for oc2 in range(2):
                c = og * 2 + oc2
                pk, pslot, pw = xspre.acquire()
                evtok = sch.step("vector", lambda e, b=bank, oc2=oc2, p=pslot:
                                 e.scalar_tensor_tensor(
                                     out=p[:, 4:4 + HF],
                                     in0=pm[b][:, oc2 * HF:(oc2 + 1) * HF],
                                     scalar=1.0, in1=rr2b,
                                     op0=ALU.mult, op1=ALU.mult),
                                 waits=[mtok] + rr_ready + pw,
                                 tag=f"xsev{k}.{h}.{c}")
                lastev = evtok
                histok = sch.step("vector", lambda e, p=pslot, c=c: e.tensor_copy(
                    out=p[:, 1:4], in_=tails[:, c * 3:(c + 1) * 3]),
                    waits=[evtok, t_z2], tag=f"hist{k}.{h}.{c}", small=True)
                sk, ss, sw = scr1.acquire()
                cv = sch.step("scalar", lambda e, p=pslot, s=ss, c=c: e.activation(
                    out=s[:, 0:HF], in_=p[:, 1:1 + HF], func=AF.Identity,
                    scale=convw[:, c * 4:c * 4 + 1]),
                    waits=[histok] + sw, tag=f"cv0.{k}.{h}.{c}")
                for kk in range(1, 4):
                    cv = sch.step("vector", lambda e, p=pslot, s=ss, c=c, kk=kk:
                                  e.scalar_tensor_tensor(
                                      out=s[:, 0:HF], in0=p[:, 1 + kk:1 + kk + HF],
                                      scalar=convw[:, c * 4 + kk:c * 4 + kk + 1],
                                      in1=s[:, 0:HF], op0=ALU.mult, op1=ALU.add),
                                  waits=[cv], tag=f"cv{kk}.{k}.{h}.{c}")
                xsw = [cv]
                if last["xssm_reader"][c]:
                    xsw.append(last["xssm_reader"][c])
                xstok = sch.step("scalar", lambda e, s=ss, c=c: e.activation(
                    out=xssm[:, c * HF:(c + 1) * HF], in_=s[:, 0:HF],
                    func=AF.Silu, bias=convb[:, c:c + 1]),
                    waits=xsw, tag=f"xsm{k}.{h}.{c}")
                scr1.release(sk, xstok)
                bk, bslot, bw = xsbf.acquire()
                cbtok = sch.step("scalar", lambda e, b=bslot, c=c: e.copy(
                    out=b, in_=xssm[:, c * HF:(c + 1) * HF]),
                    waits=[xstok] + bw, tag=f"xbf{k}.{h}.{c}")
                tltok = sch.step("vector", lambda e, p=pslot, c=c: e.tensor_copy(
                    out=tails[:, c * 3:(c + 1) * 3], in_=p[:, 1 + HF:4 + HF]),
                    waits=[evtok], tag=f"tls{k}.{h}.{c}", small=True,
                    strict=True)
                xspre.release(pk, tltok)
                last["xssm_tok"][c] = xstok
                w0 = [cbtok, xpc_tok]
                if c == 0:
                    for b2 in range(2):
                        if last["py_free"][b2]:
                            w0.append(last["py_free"][b2])
                xp1 = sch.step("tensor", lambda e, b=bslot, c=c: e.matmul(
                    out=py[0][0:DTR, 0:HF], lhsT=xpbf[:, c * 96:c * 96 + DTR],
                    rhs=b, start=(c == 0), stop=(c == CD - 1),
                    skip_group_check=True), waits=w0, tag=f"xpa{k}.{h}.{c}")
                xp2 = sch.step("tensor", lambda e, b=bslot, c=c: e.matmul(
                    out=py[1][0:32, 0:HF], lhsT=xpbf[:, c * 96 + DTR:(c + 1) * 96],
                    rhs=b, start=(c == 0), stop=(c == CD - 1),
                    skip_group_check=True), waits=[], tag=f"xpb{k}.{h}.{c}")
                xsbf.release(bk, xp2)
                if c == CD - 1:
                    ctx["xp_done"] = (xp1, xp2)
            last["pm_free"][bank] = lastev
        xp1, xp2 = ctx["xp_done"]
        drtok = sch.step("scalar", lambda e: e.copy(out=deltar,
                                                    in_=py[0][0:DTR, 0:HF]),
                         waits=[xp1], tag=f"dre{k}.{h}")
        bctok = sch.step("scalar", lambda e: e.copy(out=bcsb,
                                                    in_=py[1][0:32, 0:HF]),
                         waits=[xp2], tag=f"bce{k}.{h}")
        last["py_free"][0] = drtok
        last["py_free"][1] = bctok
        ctx["deltar_tok"] = drtok
        ctx["bc_tok"] = bctok
        ctx["gate_tok"] = {}
        emit_dt(k, h, 0)
        emit_dt(k, h, 1)
        emit_gate_og(k, h, 0)

    def emit_gate_og(k, h, g):
        bank = g % 2
        mtok = None
        for q in range(2):
            kslot, slot, wtok = mats["in"].load(q * 4, 4, 2048 + g * 256, 256)
            mmw = [wtok]
            if q == 0 and last["pm_free"][bank]:
                mmw.append(last["pm_free"][bank])

            def g_group(e, s=slot, b=bank, h=h, q=q):
                mm = None
                for kb in range(4):
                    for oc2 in range(2):
                        mm = e.matmul(
                            out=pm[b][:, oc2 * HF:(oc2 + 1) * HF],
                            lhsT=s[:, kb * 256 + oc2 * 128:
                                   kb * 256 + (oc2 + 1) * 128],
                            rhs=xnbf[:, (q * 4 + kb) * T + h * HF:
                                     (q * 4 + kb) * T + (h + 1) * HF],
                            start=(q == 0 and kb == 0 and oc2 == 0),
                            stop=(q == 1 and kb == 3),
                            skip_group_check=True)
                return mm
            mtok = sch.step("tensor", g_group, waits=mmw,
                            tag=f"gmm{k}.{h}.{g}.{q}")
            wring.release(kslot, mtok)
        lastev = None
        for oc2 in range(2):
            c = g * 2 + oc2
            sk, ss, sw = scr1.acquire()
            ev = sch.step("vector", lambda e, b=bank, oc2=oc2, s=ss:
                          e.scalar_tensor_tensor(
                              out=s[:, 0:HF],
                              in0=pm[b][:, oc2 * HF:(oc2 + 1) * HF],
                              scalar=1.0, in1=rr2b, op0=ALU.mult, op1=ALU.mult),
                          waits=[mtok] + [ctx["rr2_ready"][h * 2],
                                          ctx["rr2_ready"][h * 2 + 1]] + sw,
                          tag=f"gev{k}.{h}.{c}")
            gk, gslot, gw = gate.acquire()
            si = sch.step("scalar", lambda e, s=ss, gsl=gslot: e.activation(
                out=gsl, in_=s[:, 0:HF], func=AF.Silu), waits=[ev] + gw,
                tag=f"gsi{k}.{h}.{c}")
            scr1.release(sk, si)
            ctx["gate_tok"][c] = (gk, gslot, si)
            lastev = ev
        last["pm_free"][bank] = lastev

    def emit_bcasts(k, h):
        bw = [ctx["bc_tok"]]
        if last["bcb_reader"]:
            bw.append(last["bcb_reader"])
        readies = []
        for n in range(NST * 2):
            tgt = bbt[n] if n < NST else cbt[n - NST]
            bank = n % 2
            w1 = list(bw) + [t_idr]
            if last["py_free"][bank]:
                w1.append(last["py_free"][bank])
            m1tok = sch.step("tensor", lambda e, n=n, b=bank: e.matmul(
                out=py[b][0:1, 256:256 + HF], lhsT=identr[0:32, n:n + 1],
                rhs=bcsb, start=True, stop=True, skip_group_check=True),
                waits=w1, tag=f"bcr{k}.{h}.{n}")
            rk, rslot, rw = rowt.acquire()
            e1tok = sch.step("scalar", lambda e, b=bank, r=rslot: e.copy(
                out=r, in_=py[b][0:1, 256:256 + HF]), waits=[m1tok] + rw,
                tag=f"bcre{k}.{h}.{n}")
            m2tok = sch.step("tensor", lambda e, b=bank, r=rslot: e.matmul(
                out=py[b][:, 256:256 + HF], lhsT=onesr, rhs=r, start=True,
                stop=True, skip_group_check=True), waits=[e1tok, t_onr],
                tag=f"bcm{k}.{h}.{n}")
            rowt.release(rk, m2tok)
            e2tok = sch.step("scalar", lambda e, b=bank, tg=tgt: e.copy(
                out=tg, in_=py[b][:, 256:256 + HF]), waits=[m2tok],
                tag=f"bcb{k}.{h}.{n}")
            last["py_free"][bank] = e2tok
            readies.append(e2tok)
        ctx["bc_ready"] = readies

    def emit_scan(k, h, pump_items):
        emit_bcasts(k, h)
        bc_ready = ctx["bc_ready"]
        queue = list(pump_items)

        stage_tok = {}
        pv_last = None
        for c in range(CD):
            dk, dslot = last["delta_slot"][c]
            dtok = last["delta_tok"][c]
            uk, uslot, uw = uten.acquire()
            utok = sch.step("vector", lambda e, d=dslot, u=uslot, c=c:
                            e.tensor_tensor(out=u, in0=d,
                                            in1=xssm[:, c * HF:(c + 1) * HF],
                                            op=ALU.mult),
                            waits=[dtok, last["xssm_tok"][c]] + uw,
                            tag=f"u{k}.{h}.{c}")
            last["xssm_reader"][c] = utok
            yk, yslot, yw = ybuf.acquire()
            psk = sch.step("scalar", lambda e, y=yslot, c=c: e.activation(
                out=y, in_=xssm[:, c * HF:(c + 1) * HF], func=AF.Identity,
                scale=dsk[:, c:c + 1]),
                waits=[last["xssm_tok"][c]] + yw, tag=f"psk{k}.{h}.{c}")
            gtok_prev = psk
            sct_last = None
            pv_last = None
            for n in range(NST):
                ak, aslot, aw = dAr.acquire()
                datok = sch.step("scalar", lambda e, d=dslot, a=aslot, c=c, n=n:
                                 e.activation(out=a, in_=d, func=AF.Exp,
                                              scale=ares[:, c * NST + n:
                                                         c * NST + n + 1]),
                                 waits=[dtok, t_a2] + aw,
                                 tag=f"dA{k}.{h}.{c}.{n}")
                bk2, bslot2, bw2 = dBr.acquire()
                dbtok = sch.step("vector", lambda e, u=uslot, b=bslot2, n=n:
                                 e.tensor_tensor(out=b, in0=u, in1=bbt[n],
                                                 op=ALU.mult),
                                 waits=[utok, bc_ready[n]] + bw2,
                                 tag=f"dB{k}.{h}.{c}.{n}")
                sk2, sslot2, sw2 = srng.acquire()
                sctok = sch.step("vector", lambda e, a=aslot, b=bslot2, s=sslot2,
                                 c=c, n=n: e.tensor_tensor_scan(
                    out=s, data0=a, data1=b,
                    initial=state[:, c * NST + n:c * NST + n + 1],
                    op0=ALU.mult, op1=ALU.add),
                    waits=[datok, dbtok, t_z1] + sw2, tag=f"sc{k}.{h}.{c}.{n}")
                dAr.release(ak, sctok)
                dBr.release(bk2, sctok)
                sct_last = sctok
                pk, pslot, pw = pr.acquire()
                ptok = sch.step("vector", lambda e, s=sslot2, p=pslot, n=n:
                                e.tensor_tensor(out=p, in0=s, in1=cbt[n],
                                                op=ALU.mult),
                                waits=[sctok, bc_ready[NST + n]] + pw,
                                tag=f"p{k}.{h}.{c}.{n}")
                sttok = sch.step("vector", lambda e, s=sslot2, c=c, n=n:
                                 e.tensor_copy(
                    out=state[:, c * NST + n:c * NST + n + 1],
                    in_=s[:, HF - 1:HF]), waits=[sctok],
                    tag=f"st{k}.{h}.{c}.{n}", small=True, strict=True)
                srng.release(sk2, sttok)
                pv_last = ptok
                gtok_prev = sch.step("gpsimd", lambda e, y=yslot, p=pslot:
                                     e.tensor_tensor(out=y, in0=y, in1=p,
                                                     op=ALU.add),
                                     waits=[ptok, gtok_prev],
                                     tag=f"yA{k}.{h}.{c}.{n}")
                pr.release(pk, gtok_prev)
            ymm = gtok_prev
            uten.release(uk, pv_last)
            delta.release(dk, sct_last)
            gk, gslot, gtok = ctx["gate_tok"][c]
            ygw = [ymm, gtok]
            if h == 0:
                dst = ygstage[:, c * HF:(c + 1) * HF]
                if last["ycommit"][c]:
                    ygw.append(last["ycommit"][c])
            else:
                dst = ygt[:, c * T + HF:(c + 1) * T]
                if last["ygt_kc"][c]:
                    ygw.append(last["ygt_kc"][c])
            ygtok = sch.step("vector", lambda e, gsl=gslot, y=yslot, d=dst:
                             e.tensor_tensor(out=d, in0=y, in1=gsl,
                                             op=ALU.mult),
                             waits=ygw, tag=f"yg{k}.{h}.{c}")
            gate.release(gk, ygtok)
            ybuf.release(yk, ygtok)
            if h == 0:
                stage_tok[c] = ygtok
            else:
                ctx["ygt_tok"][c] = ygtok
            if c + 2 < CD:
                emit_dt(k, h, c + 2)
            if c % 2 == 1 and (c // 2) + 1 < 8:
                emit_gate_og(k, h, (c // 2) + 1)
            budget = 21.0
            while queue and budget > 0:
                cost, it = queue.pop(0)
                it()
                budget -= cost
        while queue:
            queue.pop(0)[1]()
        if h == 0:
            for c in range(CD):
                cw = [stage_tok[c]]
                if last["ygt_kc"][c]:
                    cw.append(last["ygt_kc"][c])
                ctok = sch.step(
                    "scalar", lambda e, c=c: e.copy(
                        out=ygt[:, c * T:c * T + HF],
                        in_=ygstage[:, c * HF:(c + 1) * HF]),
                    waits=cw, tag=f"ycommit{k}.{c}")
                ctx["ygt_tok"][c] = ctok
                last["ycommit"][c] = ctok
        last["bcb_reader"] = pv_last
        last["rr2_reader"] = pv_last

    def make_ffn(k):
        fc = {"h1in_tok": [], "h2in_tok": [], "h3_tok": [],
              "xn_rd": [], "ffnB_rd": []}

        def w1w2_oc(oc):
            def go():
                k1, s1_, t1 = mats["w1"].load(0, KD, oc * 128, 128)
                k2, s2_, t2 = mats["w2"].load(0, KD, oc * 128, 128)
                bank0, bank1 = (0, 1) if oc % 2 == 0 else (2, 3)
                w1w = [t1]
                if last["pf_free"][bank0]:
                    w1w.append(last["pf_free"][bank0])
                w2w = [t2]
                if last["pf_free"][bank1]:
                    w2w.append(last["pf_free"][bank1])
                def wgrp(e, s, b):
                    mm = None
                    for kb in range(KD):
                        mm = e.matmul(out=pf[b][:, 0:T],
                                      lhsT=s[:, kb * 128:(kb + 1) * 128],
                                      rhs=xnbf[:, kb * T:(kb + 1) * T],
                                      start=(kb == 0), stop=(kb == KD - 1),
                                      skip_group_check=True)
                    return mm
                m1tok = sch.step("tensor", lambda e, s=s1_, b=bank0: wgrp(e, s, b),
                                 waits=w1w, tag=f"w1mm{k}.{oc}")
                m2tok = sch.step("tensor", lambda e, s=s2_, b=bank1: wgrp(e, s, b),
                                 waits=w2w, tag=f"w2mm{k}.{oc}")
                fc["xn_rd"].append(m2tok)
                wring.release(k1, m1tok)
                wring.release(k2, m2tok)
                sk, ss, sw = scr1.acquire()
                s1tok = sch.step("scalar", lambda e, s=ss, b=bank0, oc=oc:
                                 e.activation(out=s, in_=pf[b][:, 0:T],
                                              func=AF.Silu,
                                              scale=s1c[:, oc:oc + 1]),
                                 waits=[m1tok] + sw, tag=f"m1.{k}.{oc}")
                wA = [m2tok, s1tok]
                if last["ffnA_reader"]:
                    wA.append(last["ffnA_reader"])
                htok = sch.step("vector", lambda e, s=ss, b=bank1, oc=oc:
                                e.scalar_tensor_tensor(
                                    out=ffnA[:, oc * T:(oc + 1) * T],
                                    in0=pf[b][:, 0:T], scalar=s2c[:, oc:oc + 1],
                                    in1=s, op0=ALU.mult, op1=ALU.mult),
                                waits=wA, tag=f"h1in{k}.{oc}")
                scr1.release(sk, htok)
                last["pf_free"][bank0] = s1tok
                last["pf_free"][bank1] = htok
                fc["h1in_tok"].append(htok)
            return go

        def hmat_pair(mat, src, dst, scales, oc0, out_list, dep_list, rd_list,
                      wrguard, banks, free_key):
            def go():
                mtoks = []
                for i in range(2):
                    oc = oc0 + i
                    bank = oc % len(banks)

                    def hgrp(e, s, b, koff, first, last_):
                        mm = None
                        for kk in range(8):
                            mm = e.matmul(out=banks[b][:, 0:T],
                                          lhsT=s[:, kk * 128:(kk + 1) * 128],
                                          rhs=src[:, (koff + kk) * T:
                                                   (koff + kk + 1) * T],
                                          start=(first and kk == 0),
                                          stop=(last_ and kk == 7),
                                          skip_group_check=True)
                        return mm
                    mtok = None
                    for q in range(4):
                        ka, sa, ta = mats[mat].load(q * 8, 8, oc * 128, 128)
                        w0 = [ta]
                        if q == 0:
                            w0 += list(fc[dep_list])
                            if last[free_key][bank]:
                                w0.append(last[free_key][bank])
                        mtok = sch.step("tensor", lambda e, s=sa, b=bank, q=q:
                                        hgrp(e, s, b, q * 8, q == 0, q == 3),
                                        waits=w0, tag=f"{mat}{q}.{k}.{oc}")
                        wring.release(ka, mtok)
                    mtoks.append(mtok)
                    if rd_list is not None:
                        fc[rd_list].append(mtok)
                        last["ffnB_reader"] = mtok
                for i in range(2):
                    oc = oc0 + i
                    bank = oc % len(banks)
                    wE = [mtoks[i]]
                    if wrguard is not None and last[wrguard]:
                        wE.append(last[wrguard])
                    etok = sch.step("scalar", lambda e, b=bank, oc=oc:
                                    e.activation(
                        out=dst[:, oc * T:(oc + 1) * T], in_=banks[b][:, 0:T],
                        func=AF.Gelu, scale=scales[:, oc:oc + 1]),
                        waits=wE, tag=f"{mat}e{k}.{oc}")
                    last[free_key][bank] = etok
                    fc[out_list].append(etok)
            return go

        w12h1 = [w1w2_oc(oc) for oc in range(KH)]
        w12h1 += [hmat_pair("h1", ffnA, ffnB, h1sc, oc0, "h2in_tok", "h1in_tok",
                            None, "ffnB_reader", pf, "pf_free")
                  for oc0 in range(0, KH, 2)]
        h2s = [hmat_pair("h2", ffnB, ffnA, h2sc, oc0, "h3_tok", "h2in_tok",
                         "ffnB_rd", "ffnA_reader", py, "py_free")
               for oc0 in range(0, KH, 2)]
        return fc, w12h1, h2s

    def make_tail(k, tctx):
        par = k % 2
        fc = tctx["fc"]
        items = []

        def oh_items(oh):
            res = []
            mms = []

            def go_out(kcg):
                def go():
                    mtok = None
                    for q in range(2):
                        kslot, slot, wtok = mats["out"].load(
                            kcg * 4 + q * 2, 2, oh * 512, 512)
                        w0 = [wtok]
                        for i in range(2):
                            w0 += list(tctx["ygt_tok"][kcg * 4 + q * 2 + i])
                        if kcg == 0 and q == 0:
                            for ts in range(4):
                                if last["pf_free"][ts]:
                                    w0.append(last["pf_free"][ts])

                        def ogrp(e, s=slot, kcg=kcg, q=q):
                            mm = None
                            for kc2 in range(2):
                                kc = kcg * 4 + q * 2 + kc2
                                for ts in range(4):
                                    mm = e.matmul(
                                        out=pf[ts][:, 0:512],
                                        lhsT=ygt[:, kc * T + ts * 128:
                                                 kc * T + (ts + 1) * 128],
                                        rhs=s[:, kc2 * 512:(kc2 + 1) * 512],
                                        start=(kc == 0), stop=False,
                                        skip_group_check=True)
                            return mm
                        mtok = sch.step("tensor", ogrp, waits=w0,
                                        tag=f"op{k}.{oh}.{kcg}.{q}")
                        if oh == 1:
                            for i in range(2):
                                last["ygt_kc"][kcg * 4 + q * 2 + i] = mtok
                        wring.release(kslot, mtok)
                        mms.append(mtok)
                return go

            def go_w3(kcg):
                def go():
                    mtok = None
                    for q in range(2):
                        kslot, slot, wtok = mats["w3"].load(
                            kcg * 4 + q * 2, 2, oh * 512, 512)
                        w0 = [wtok] + [fc["h3_tok"][kcg * 4 + q * 2 + i]
                                       for i in range(2)]

                        def wgrp3(e, s=slot, kcg=kcg, q=q):
                            mm = None
                            for kc2 in range(2):
                                kc = kcg * 4 + q * 2 + kc2
                                for ts in range(4):
                                    mm = e.matmul(
                                        out=pf[ts][:, 0:512],
                                        lhsT=ffnA[:, kc * T + ts * 128:
                                                  kc * T + (ts + 1) * 128],
                                        rhs=s[:, kc2 * 512:(kc2 + 1) * 512],
                                        start=False,
                                        stop=(kcg == 7 and q == 1 and kc2 == 1),
                                        skip_group_check=True)
                            return mm
                        mtok = sch.step("tensor", wgrp3, waits=w0,
                                        tag=f"w3.{k}.{oh}.{kcg}.{q}")
                        if oh == 1:
                            last["ffnA_reader"] = mtok
                        wring.release(kslot, mtok)
                        mms.append(mtok)
                return go

            def go_evac():
                def go():
                    for ts in range(4):
                        row0 = k * T + ts * 128
                        ek, eslot, ew = exr.acquire()
                        xtok = sch.step("scalar", lambda e, d=eslot,
                                        a=x_hbm[row0:row0 + 128,
                                                oh * 512:(oh + 1) * 512]:
                                        e.dma_start(out=d, in_=a), waits=ew,
                                        sem=f"EX{ek}", inc=16,
                                        tag=f"ex{k}.{ts}.{oh}")
                        ok_, oslot, ow = outr.acquire()
                        v1 = sch.step("vector", lambda e, x2=eslot, o=oslot,
                                      ts=ts: e.scalar_tensor_tensor(
                            out=o, in0=x2,
                            scalar=stats[:, par * 16 + ts:par * 16 + ts + 1],
                            in1=pf[ts][:, 0:512], op0=ALU.mult, op1=ALU.add),
                            waits=[xtok] + list(mms) + ow,
                            tag=f"ev1.{k}.{ts}.{oh}")
                        exr.release(ek, v1)
                        last["pf_free"][ts] = v1
                        v2 = sch.step("scalar", lambda e, o=oslot, ts=ts:
                                      e.activation(
                            out=o, in_=o, func=AF.Identity,
                            bias=stats[:, par * 16 + 4 + ts:
                                       par * 16 + 4 + ts + 1]),
                            waits=[v1], tag=f"ev2.{k}.{ts}.{oh}")
                        otok = dma(out_hbm[row0:row0 + 128,
                                           oh * 512:(oh + 1) * 512],
                                   oslot, waits=[v2], sem=f"DO{ok_}",
                                   tag=f"o{k}.{ts}.{oh}")
                        outr.release(ok_, otok)
                        tctx["out_toks"].append(otok)
                    mms.clear()
                return go

            res = [go_out(g) for g in range(4)] + [go_w3(g) for g in range(8)]
            res.append(go_evac())
            return res

        items += oh_items(0)
        items += oh_items(1)
        return items

    def weave(h2p, tailq):
        h2p = [(14.0, f) for f in h2p]
        tailq = [(5.0, f) for f in tailq]
        q = []
        hi = 0
        for j in range(4):
            q += [h2p[hi], h2p[hi + 1], tailq[j]]
            hi += 2
        for j in range(8):
            if hi < len(h2p):
                q += [h2p[hi], h2p[hi + 1]]
                hi += 2
            q.append(tailq[4 + j])
        q += h2p[hi:]
        q += tailq[12:]
        return q

    prev_tctx = None
    pending_h2 = []
    for k in range(nchunks):
        ctx = {"ygt_tok": {}}
        emit_head(k, 0)
        emit_mix(k, 0)
        pump = []
        if prev_tctx is not None:
            pump = weave(pending_h2, make_tail(k - 1, prev_tctx))
        elif pending_h2:
            pump = [(14.0, f) for f in pending_h2]
        pending_h2 = []
        emit_scan(k, 0, pump_items=pump)
        if prev_tctx is not None:
            last["epi"] = prev_tctx["out_toks"][-1]
        commitsA = dict(ctx["ygt_tok"])
        ctx = {"ygt_tok": {}}
        emit_head(k, 1)
        emit_mix(k, 1)
        fc, w12h1, h2items = make_ffn(k)
        costed = [(3.5, f) for f in w12h1[:KH]] + [(14.0, f) for f in w12h1[KH:]]
        emit_scan(k, 1, pump_items=costed)
        pending_h2 = h2items
        last["xn_reader"] = fc["xn_rd"][-1]
        tctx = {"ygt_tok": {c: [commitsA[c], ctx["ygt_tok"][c]]
                            for c in range(CD)},
                "fc": fc, "out_toks": []}
        prev_tctx = tctx

    for it in pending_h2:
        it()
    for it in make_tail(nchunks - 1, prev_tctx):
        it()

    tapspec = os.environ.get("MKTAPS", "")
    if tapspec:
        allt = dict(xnbf=xnbf, xssm=xssm, ygt=ygt, ffnA=ffnA, ffnB=ffnB,
                    state=state, stats=stats, rr2b=rr2b, ares=ares,
                    deltar=deltar, bcsb=bcsb, w3sb=w3sb, xpbf=xpbf, dtbf=dtbf,
                    convw=convw, normw=normw, tails=tails)
        for n_ in range(NST):
            allt[f"bb{n_}"] = bbt[n_]
            allt[f"cb{n_}"] = cbt[n_]
        eng_fin = [(s, sch.total[s]) for s in ("T", "V", "S") if s in sch.total]
        for tn in tapspec.split(","):
            ap = allt[tn]
            dt_ = ap.tensor.dtype
            tout = nc.dram_tensor("tap_" + tn, list(ap.shape), dt_,
                                  kind="ExternalOutput").ap()
            sch.step("sync", lambda e, a=ap, o=tout: e.dma_start(out=o, in_=a),
                     waits=eng_fin, sem="DO0", tag=f"tap{tn}")
    eng_sems = {"T", "V", "S", "G"}
    fin_waits = [(s, sch.total[s]) for s in sorted(sch.total)
                 if s not in eng_sems]
    sch.step("sync", lambda e: e.nop(), waits=fin_waits, sem="DW", tag="fin")

    with nc.Block() as block:
        sch.emit(nc, block)
    return nc


def _prep(inputs):
    f = np.float32
    x = np.ascontiguousarray(np.asarray(inputs["L0_attn_x"], dtype=f))
    w_in_t = np.ascontiguousarray(np.asarray(inputs["L0_attn_in_proj_w"], f).T)
    w_out_t = np.ascontiguousarray(np.asarray(inputs["L0_attn_out_proj_w"], f).T)
    xproj = np.asarray(inputs["L0_attn_x_proj_w"], f)
    w_xp_fm = np.zeros((128, CD * 96), dtype=f)
    for c in range(CD):
        w_xp_fm[:, c * 96:(c + 1) * 96] = xproj[:, c * 128:(c + 1) * 128].T
    w_dt_t = np.ascontiguousarray(np.asarray(inputs["L0_attn_dt_proj_w"], f).T)
    w1_t = np.ascontiguousarray(np.asarray(inputs["L0_ffn_w1_w"], f).T)
    w2_t = np.ascontiguousarray(np.asarray(inputs["L0_ffn_w2_w"], f).T)
    h1_t = np.ascontiguousarray(np.asarray(inputs["L0_ffn_h1_w"], f).T)
    h2_t = np.ascontiguousarray(np.asarray(inputs["L0_ffn_h2_w"], f).T)
    w3_t = np.ascontiguousarray(np.asarray(inputs["L0_ffn_w3_w"], f).T)

    def fmcols(v, ncol):
        return np.ascontiguousarray(np.asarray(v, f).reshape(ncol, 128).T)

    conv = np.asarray(inputs["L0_attn_conv_w"], f).reshape(DI, 4)
    conv_fm = np.zeros((128, CD * 4), dtype=f)
    for c in range(CD):
        conv_fm[:, c * 4:(c + 1) * 4] = conv[c * 128:(c + 1) * 128, :]
    alog = np.asarray(inputs["L0_attn_A_log"], f)
    alog_fm = np.zeros((128, CD * NST), dtype=f)
    for c in range(CD):
        alog_fm[:, c * NST:(c + 1) * NST] = alog[c * 128:(c + 1) * 128, :]

    common = dict(
        w_in_t=w_in_t, w_out_t=w_out_t, w_xp_fm=w_xp_fm, w_dt_t=w_dt_t,
        w1_t=w1_t, w2_t=w2_t, h1_t=h1_t, h2_t=h2_t, w3_t=w3_t,
        conv_fm=conv_fm,
        convb_fm=fmcols(inputs["L0_attn_conv_b"], CD),
        dtb_fm=fmcols(inputs["L0_attn_dt_proj_b"], CD),
        dsk_fm=fmcols(inputs["L0_attn_D_skip"], CD),
        alog_fm=alog_fm,
        norm_fm=fmcols(inputs["L0_attn_norm_w"], KD),
        s1_fm=fmcols(inputs["L0_ffn_w1_s"], KH),
        s2_fm=fmcols(inputs["L0_ffn_w2_s"], KH),
        h1s_fm=fmcols(inputs["L0_ffn_h1_s"], KH),
        h2s_fm=fmcols(inputs["L0_ffn_h2_s"], KH),
        w3s_row=np.ascontiguousarray(np.asarray(inputs["L0_ffn_w3_s"], f)
                                     .reshape(1, D)),
        ident_in=np.eye(128, dtype=f),
        ones_in=np.ones((1, 128), dtype=f),
    )
    in_maps = []
    for b in range(8):
        m = dict(common)
        m["x"] = np.ascontiguousarray(x[b])
        in_maps.append(m)
    return in_maps


def kernel(**inputs):
    in_maps = _prep(inputs)
    nc = build()
    res = run_bass_kernel_spmd(nc, in_maps, core_ids=list(range(8)))
    out = np.stack([np.asarray(r["out"], dtype=np.float32)
                    for r in res.results], axis=0)
    return out


# version 13
# speedup vs baseline: 1.0060x; 1.0060x over previous
import os
import numpy as np
import concourse.bass as bass
import concourse.mybir as mybir
from concourse.bass_utils import run_bass_kernel_spmd

F32 = mybir.dt.float32
F32R = mybir.dt.float32r
BF16 = mybir.dt.bfloat16
AF = mybir.ActivationFunctionType
ALU = mybir.AluOpType

D = 1024
DI = 2048
L = 4096
NCH = 8
T = 512
HF = 256
KD = 8
CD = 16
NST = 16
DTR = 64
HID = 4096
KH = 32
EPS = 1e-5


class Sched:
    def __init__(self):
        self.steps = []
        self.total = {}
        self.sem_names = set()
        self.small_toks = set()

    def step(self, engine, fn, waits=(), sem=None, inc=None, tag="",
             small=False, strict=False):
        if sem is None:
            sem = {"sync": "DW", "tensor": "T", "vector": "V",
                   "scalar": "S", "gpsimd": "G"}[engine]
        if inc is None:
            inc = 16 if engine == "sync" else 1
        for w in waits:
            assert w is not None, f"None wait in {tag}"
            (s, v) = w
            assert s in self.total and v <= self.total[s], \
                f"wait on future token {s}={v} (now {self.total.get(s)}) tag={tag}"
        self.total[sem] = self.total.get(sem, 0) + inc
        self.sem_names.add(sem)
        self.steps.append((engine, fn, sem, inc, tuple(waits), tag, strict))
        tok = (sem, self.total[sem])
        if small:
            self.small_toks.add(tok)
        return tok

    def emit(self, nc, block):
        sems = {name: nc.alloc_semaphore("sem_" + name)
                for name in sorted(self.sem_names)}
        own = {"sync": None, "tensor": "T", "vector": "V", "scalar": "S",
               "gpsimd": "G"}

        def run(eng_name, eng):
            waited = {}
            for (engine, fn, sem, inc, waits, tag, strict) in self.steps:
                if engine != eng_name:
                    continue
                for (s, v) in waits:
                    if (s == own[eng_name] and not STRICT and not strict
                            and (s, v) not in self.small_toks):
                        continue
                    if waited.get(s, 0) >= v:
                        continue
                    eng.wait_ge(sems[s], v)
                    waited[s] = v
                last = fn(eng)
                last.then_inc(sems[sem], inc)

        @block.sync
        def _(eng):
            run("sync", eng)

        @block.tensor
        def _(eng):
            run("tensor", eng)

        @block.vector
        def _(eng):
            run("vector", eng)

        @block.scalar
        def _(eng):
            run("scalar", eng)

        if any(s[0] == "gpsimd" for s in self.steps):
            @block.gpsimd
            def _(eng):
                run("gpsimd", eng)


class Ring:
    def __init__(self, aps):
        self.aps = list(aps)
        self.free_tok = [None] * len(self.aps)
        self.i = 0

    def acquire(self):
        k = self.i % len(self.aps)
        self.i += 1
        tok = self.free_tok[k]
        return k, self.aps[k], ([tok] if tok else [])

    def release(self, k, token):
        self.free_tok[k] = token


STRICT = os.environ.get("MKRC", "0") == "1"


def build(nchunks=NCH):
    nc = bass.Bass("TRN2", target_bir_lowering=False,
                   detect_race_conditions=STRICT)

    def din(name, shape, dtype=F32):
        return nc.dram_tensor(name, shape, dtype, kind="ExternalInput").ap()

    x_hbm = din("x", [L, D])
    w_in_t = din("w_in_t", [D, 2 * DI])
    w_out_t = din("w_out_t", [DI, D])
    w_xp_fm = din("w_xp_fm", [128, CD * 96])
    w_dt_t = din("w_dt_t", [DTR, DI])
    w1_t = din("w1_t", [D, HID])
    w2_t = din("w2_t", [D, HID])
    h1_t = din("h1_t", [HID, HID])
    h2_t = din("h2_t", [HID, HID])
    w3_t = din("w3_t", [HID, D])
    conv_fm = din("conv_fm", [128, CD * 4])
    convb_fm = din("convb_fm", [128, CD])
    dtb_fm = din("dtb_fm", [128, CD])
    dsk_fm = din("dsk_fm", [128, CD])
    alog_fm = din("alog_fm", [128, CD * NST])
    norm_fm = din("norm_fm", [128, KD])
    s1_fm = din("s1_fm", [128, KH])
    s2_fm = din("s2_fm", [128, KH])
    h1s_fm = din("h1s_fm", [128, KH])
    h2s_fm = din("h2s_fm", [128, KH])
    w3s_row = din("w3s_row", [1, D])
    ident_in = din("ident_in", [128, 128])
    ones_in = din("ones_in", [1, 128])

    out_hbm = nc.dram_tensor("out", [L, D], F32, kind="ExternalOutput").ap()

    scr = {}
    for nm, ntiles, tcols in [
        ("in", 16, 2048), ("out", 8, 2048), ("w1", 32, 1024), ("w2", 32, 1024),
        ("h1", 64, 2048), ("h2", 64, 2048), ("w3", 16, 2048),
    ]:
        scr[nm] = nc.dram_tensor("scr_" + nm, [ntiles * 128, tcols], BF16,
                                 kind="Internal").ap()

    def sb(name, shape, dtype=F32):
        return nc.alloc_sbuf_tensor(name, shape, dtype).ap()

    xr = Ring([sb(f"xr{i}", [128, 1024]) for i in range(1)])
    exr = Ring([sb(f"exr{i}", [128, 512]) for i in range(1)])
    stats = sb("stats", [128, 32])
    xnbf = sb("xnbf", [128, KD * T], BF16)
    xssm = sb("xssm", [128, CD * HF])
    xspre = Ring([sb(f"xspre{i}", [128, HF + 4]) for i in range(2)])
    xsbf = Ring([sb(f"xsbf{i}", [128, HF], BF16) for i in range(1)])
    delta = Ring([sb(f"delta{i}", [128, HF]) for i in range(2)])
    uten = Ring([sb("uten0", [128, HF])])
    dAr = Ring([sb(f"dA{i}", [128, HF]) for i in range(2)])
    dBr = Ring([sb(f"dB{i}", [128, HF]) for i in range(1)])
    srng = Ring([sb(f"sr{i}", [128, HF]) for i in range(1)])
    pr = Ring([sb(f"pr{i}", [128, HF]) for i in range(4)])
    ybuf = Ring([sb(f"ybuf{i}", [128, HF]) for i in range(2)])
    bbt = [sb(f"bb{n}", [128, HF]) for n in range(NST)]
    cbt = [sb(f"cb{n}", [128, HF]) for n in range(NST)]
    gate = Ring([sb(f"gate{i}", [128, HF]) for i in range(2)])
    ygt = sb("ygt", [128, CD * T], BF16)
    ygstage = sb("ygstage", [128, CD * HF], BF16)
    ffnA = sb("ffnA", [128, KH * T], BF16)
    ffnB = sb("ffnB", [128, KH * T], BF16)
    scr1 = Ring([sb(f"scr1_{i}", [128, 512]) for i in range(2)])
    fring = Ring([sb(f"fr{i}", [128, 512]) for i in range(1)])
    hstat = Ring([sb(f"hstat{i}", [128, 24]) for i in range(2)])
    outr = Ring([sb(f"outr{i}", [128, 512]) for i in range(1)])
    wring = Ring([sb(f"wr{i}", [128, 2048], BF16) for i in range(3)])
    state = sb("state", [128, CD * NST])
    xpbf = sb("xpbf", [128, CD * 96], BF16)
    dtbf = sb("dtbf", [DTR, DI], BF16)
    deltar = sb("deltar", [DTR, HF], BF16)
    bcsb = sb("bcsb", [32, HF], F32R)
    ares = sb("ares", [128, CD * NST])
    convw = sb("convw", [128, CD * 4])
    convb = sb("convb", [128, CD])
    dtb = sb("dtb", [128, CD])
    dsk = sb("dsk", [128, CD])
    normw = sb("normw", [128, KD])
    s1c = sb("s1c", [128, KH])
    s2c = sb("s2c", [128, KH])
    h1sc = sb("h1sc", [128, KH])
    h2sc = sb("h2sc", [128, KH])
    w3sb = sb("w3sb", [128, D])
    identf = sb("identf", [128, 128])
    identr = sb("identr", [128, 128], F32R)
    onesr = sb("onesr", [1, 128], F32R)
    rowt = Ring([sb(f"rowt{i}", [1, HF], F32R) for i in range(1)])
    rr2b = sb("rr2b", [128, HF])
    lnscr = sb("lnscr", [128, 128])
    tails = sb("tails", [128, CD * 3])

    def psalloc(name):
        return nc.alloc_psum_tensor(name, [128, 512], F32).ap()

    pm = [psalloc("pm0"), psalloc("pm1")]
    pf = [psalloc(f"pf{i}") for i in range(4)]
    py = [psalloc("py0"), psalloc("py1")]

    sch = Sched()

    def dma(dst, src, waits=(), tag="", sem="DW"):
        return sch.step("sync", lambda e: e.dma_start(out=dst, in_=src),
                        waits=waits, sem=sem, tag=tag)

    for dst, src_ in [
        (convw, conv_fm), (convb, convb_fm), (dtb, dtb_fm), (dsk, dsk_fm),
        (ares, alog_fm), (normw, norm_fm), (s1c, s1_fm), (s2c, s2_fm),
        (h1sc, h1s_fm), (identf, ident_in), (h2sc, h2s_fm),
        (w3sb[0:1, :], w3s_row),
    ]:
        dma(dst, src_, tag="p:load", sem="DW")
    ptok_all = ("DW", sch.total["DW"])
    ptoks = [ptok_all]
    fk0, fs0, fw0 = fring.acquire()
    t_of = dma(fs0[0:1, 0:128], ones_in, waits=fw0, sem="FR0", tag="p:onesf")
    t_onr2 = sch.step("scalar", lambda e, s=fs0: e.copy(out=onesr,
                                                        in_=s[0:1, 0:128]),
                      waits=[t_of], tag="p:onr2")
    fring.release(fk0, t_onr2)
    t_a1 = sch.step("scalar", lambda e: e.activation(out=ares, in_=ares, func=AF.Exp),
                    waits=ptoks, tag="p:expA")
    t_a2 = sch.step("scalar", lambda e: e.activation(out=ares, in_=ares,
                                                     func=AF.Identity, scale=-1.0),
                    waits=[t_a1], tag="p:negA")
    t_idr = sch.step("scalar", lambda e: e.copy(out=identr, in_=identf),
                     waits=ptoks, tag="p:idr")
    t_onr = t_onr2
    w3sb_tok = None
    last_pm_seed = [None, None]
    for j in range(4):
        rk, rs, rw = rowt.acquire()
        tc = sch.step("scalar", lambda e, r=rs, j=j: e.copy(
            out=r, in_=w3sb[0:1, j * 256:(j + 1) * 256]),
            waits=ptoks + rw + ([w3sb_tok] if w3sb_tok else []), tag=f"p:w3r{j}")
        tm = sch.step("tensor", lambda e, r=rs, j=j: e.matmul(
            out=pm[j % 2][:, 0:256], lhsT=onesr, rhs=r, start=True, stop=True,
            skip_group_check=True), waits=[tc, t_onr], tag=f"p:w3m{j}")
        rowt.release(rk, tm)
        w3sb_tok = sch.step("scalar", lambda e, j=j: e.copy(
            out=w3sb[:, j * 256:(j + 1) * 256], in_=pm[j % 2][:, 0:256]),
            waits=[tm], tag=f"p:w3e{j}")
        last_pm_seed[j % 2] = w3sb_tok

    xp_toks = []
    for j in range(3):
        fk, fs, fw = fring.acquire()
        dtok = dma(fs[:, 0:512], w_xp_fm[:, j * 512:(j + 1) * 512],
                   waits=fw + [ptok_all], sem=f"FR{fk}", tag=f"p:xpf{j}")
        ctok = sch.step("scalar", lambda e, s=fs, jj=j: e.copy(
            out=xpbf[:, jj * 512:(jj + 1) * 512], in_=s[:, 0:512]),
            waits=[dtok], tag=f"p:xpc{j}")
        fring.release(fk, ctok)
        xp_toks.append(ctok)
    for j in range(4):
        fk, fs, fw = fring.acquire()
        dtok = dma(fs[0:DTR, 0:512], w_dt_t[:, j * 512:(j + 1) * 512],
                   waits=fw, sem=f"FR{fk}", tag=f"p:dtf{j}")
        ctok = sch.step("scalar", lambda e, s=fs, jj=j: e.copy(
            out=dtbf[:, jj * 512:(jj + 1) * 512], in_=s[0:DTR, 0:512]),
            waits=[dtok], tag=f"p:dtc{j}")
        fring.release(fk, ctok)
        xp_toks.append(ctok)
    t_z1 = sch.step("vector", lambda e: e.memset(state, 0.0), tag="p:zst")
    t_z2 = sch.step("vector", lambda e: e.memset(tails, 0.0), tag="p:ztl")
    xpc_tok = xp_toks[-1]

    class WMat:

        def __init__(self, name, src, scratch, scale_cols=None, w3_scale=False):
            self.name = name
            self.src = src
            self.scratch = scratch
            self.scale_cols = scale_cols
            self.w3_scale = w3_scale
            self.seen = {}
            self.next_slot = 0

        def convert(self, k0, nk, o0, no):
            key = (k0, nk, o0, no)
            assert key not in self.seen
            ti = self.next_slot
            self.next_slot += 1
            bk, bslot, bw = cvb.acquire()
            cast_toks = []
            nb = max(1, 512 // no)
            kb = 0
            while kb < nk:
                g = min(nb, nk - kb)
                src3 = self.src[(k0 + kb) * 128:(k0 + kb + g) * 128, o0:o0 + no]
                fk, fs, fw = cvf.acquire()
                if g > 1:
                    src_ap = src3.rearrange("(b p) o -> p b o", p=128)
                    dst_ap = fs.rearrange("p (b o) -> p b o", b=g)
                else:
                    src_ap = src3
                    dst_ap = fs[:, 0:no]
                dtok = sch.step("sync", lambda e, d=dst_ap, a=src_ap:
                                e.dma_start(out=d, in_=a),
                                waits=fw + bw, sem=f"CF{fk}",
                                tag=f"{self.name}cf{key}.{kb}")
                bw = []
                tgt = bslot[:, kb * no:(kb + g) * no]
                if self.scale_cols is not None:
                    ct = dtok
                    for b2 in range(g):
                        kk = k0 + kb + b2
                        ct = sch.step("gpsimd", lambda e, s=fs, b2=b2, kk=kk,
                                      kb=kb, no=no, bs=bslot:
                                      e.tensor_scalar(
                            out=bs[:, (kb + b2) * no:(kb + b2 + 1) * no],
                            in0=s[:, b2 * no:(b2 + 1) * no],
                            scalar1=self.scale_cols[:, kk:kk + 1], scalar2=0.0,
                            op0=ALU.mult, op1=ALU.add),
                            waits=[dtok], tag=f"{self.name}cv{key}.{kb}.{b2}")
                    ctok = ct
                elif self.w3_scale:
                    ctok = sch.step("gpsimd", lambda e, s=fs, t=tgt, g=g, no=no:
                                    e.tensor_tensor(
                        out=t, in0=s[:, 0:g * no],
                        in1=w3sb[:, o0:o0 + no], op=ALU.mult),
                        waits=[dtok, w3sb_tok], tag=f"{self.name}cv{key}.{kb}")
                else:
                    ctok = sch.step("gpsimd", lambda e, s=fs, t=tgt, g=g, no=no:
                                    e.tensor_copy(out=t, in_=s[:, 0:g * no]),
                                    waits=[dtok], tag=f"{self.name}cv{key}.{kb}")
                cvf.release(fk, ctok)
                cast_toks.append(ctok)
                kb += g
            wbt = sch.step("sync", lambda e, s=bslot, nn=nk * no, t=ti:
                           e.dma_start(out=self.scratch[t * 128:(t + 1) * 128,
                                                        0:nn],
                                       in_=s[:, 0:nn]),
                           waits=cast_toks, sem=f"CB{bk}",
                           tag=f"{self.name}cwb{key}")
            cvb.release(bk, wbt)
            self.seen[key] = (ti, wbt)

        def load(self, k0, nk, o0, no, waits=()):
            key = (k0, nk, o0, no)
            kslot, slot, aw = wring.acquire()
            ti, wbt = self.seen[key]
            src = self.scratch[ti * 128:(ti + 1) * 128, 0:nk * no]
            tok = sch.step("sync", lambda e, s=slot, a=src, nn=nk * no:
                           e.dma_start(out=s[:, 0:nn], in_=a),
                           waits=list(waits) + aw + [wbt],
                           sem=f"WR{kslot}", tag=f"{self.name}ld{key}")
            return kslot, slot, tok

    mats = {
        "in": WMat("in", w_in_t, scr["in"], scale_cols=normw),
        "out": WMat("out", w_out_t, scr["out"]),
        "w1": WMat("w1", w1_t, scr["w1"]),
        "w2": WMat("w2", w2_t, scr["w2"]),
        "h1": WMat("h1", h1_t, scr["h1"]),
        "h2": WMat("h2", h2_t, scr["h2"]),
        "w3": WMat("w3", w3_t, scr["w3"], w3_scale=True),
    }

    ffnA_f32 = ffnA.bitcast(F32)
    ffnB_b16 = ffnB
    cvf = Ring([ffnA_f32[:, i * 512:(i + 1) * 512] for i in range(8)])
    cvb = Ring([ffnB_b16[:, i * 2048:(i + 1) * 2048] for i in range(4)])
    for og in range(16):
        mats["in"].convert(0, KD, og * 256, 256)
    for oc in range(KH):
        mats["w1"].convert(0, KD, oc * 128, 128)
        mats["w2"].convert(0, KD, oc * 128, 128)
    for oc in range(KH):
        mats["h1"].convert(0, 16, oc * 128, 128)
        mats["h1"].convert(16, 16, oc * 128, 128)
    for oc in range(KH):
        mats["h2"].convert(0, 16, oc * 128, 128)
        mats["h2"].convert(16, 16, oc * 128, 128)
    for oh in range(2):
        for kcg in range(4):
            mats["out"].convert(kcg * 4, 4, oh * 512, 512)
        for kcg in range(8):
            mats["w3"].convert(kcg * 4, 4, oh * 512, 512)
    conv_toks = [t for t in cvf.free_tok + cvb.free_tok if t]
    conv_done = sch.step("gpsimd", lambda e: e.nop(), waits=conv_toks,
                         tag="p:convdone")
    conv_fa = conv_done
    conv_fb = conv_done

    last = {
        "xn_reader": None,
        "ygt_kc": [None] * CD,
        "ffnA_reader": None,
        "ffnB_reader": None,
        "bcb_reader": None,
        "rr2_reader": None,
        "pm_free": [None, None],
        "pf_free": [None, None, None, None],
        "py_free": [None, None],
        "delta_tok": [None] * CD,
        "delta_slot": [None] * CD,
        "xssm_tok": [None] * CD,
        "xssm_reader": [None] * CD,
        "epi": None,
        "ycommit": [None] * CD,
    }
    last["ffnA_reader"] = conv_fa
    last["ffnB_reader"] = conv_fb
    last["pm_free"] = list(last_pm_seed)

    ctx = {}

    def emit_head(k, h):
        par = k % 2
        ctx["rr2_ready"] = {}
        ctx["xn_ready"] = []
        for i in range(2):
            ts = h * 2 + i
            row0 = k * T + ts * 128
            kx, xs_, xw = xr.acquire()
            xtok = sch.step("scalar", lambda e, d=xs_, a=x_hbm[row0:row0 + 128, :]:
                            e.dma_start(out=d, in_=a), waits=xw,
                            sem=f"XR{kx}", inc=16, tag=f"x{k}.{ts}")
            sk, ss, sw = hstat.acquire()
            t1a = sch.step("scalar", lambda e, s=ss, x2=xs_: e.activation(
                out=x2, in_=x2, func=AF.Identity, accum_out=s[:, 18:19]),
                waits=[xtok] + sw, tag=f"sum{k}.{ts}")
            sq_prev = t1a
            for qi in range(8):
                sq_prev = sch.step("scalar", lambda e, s=ss, x2=xs_, qi=qi:
                                   e.activation(
                    out=lnscr, in_=x2[:, qi * 128:(qi + 1) * 128],
                    func=AF.Square, accum_out=s[:, 19 + (qi > 0) * 3:
                                                20 + (qi > 0) * 3]),
                    waits=[sq_prev], tag=f"sq{qi}.{k}.{ts}")
                if qi > 0:
                    sq_prev = sch.step("vector", lambda e, s=ss:
                                       e.tensor_tensor(
                        out=s[:, 19:20], in0=s[:, 19:20], in1=s[:, 22:23],
                        op=ALU.add), waits=[sq_prev],
                        tag=f"sqa{qi}.{k}.{ts}", small=True)
            t1b = sq_prev
            t2a = sch.step("vector", lambda e, s=ss: e.tensor_scalar(
                out=s[:, 8:9], in0=s[:, 18:19], scalar1=1.0 / D, scalar2=0.0,
                op0=ALU.mult, op1=ALU.add), waits=[t1a], tag=f"mn{k}.{ts}", small=True)
            t2b = sch.step("vector", lambda e, s=ss: e.tensor_tensor(
                out=s[:, 20:21], in0=s[:, 8:9], in1=s[:, 8:9], op=ALU.mult),
                waits=[t2a], tag=f"mn2{k}.{ts}", small=True)
            t2c = sch.step("vector", lambda e, s=ss: e.tensor_scalar(
                out=s[:, 21:22], in0=s[:, 19:20], scalar1=1.0 / D, scalar2=0.0,
                op0=ALU.mult, op1=ALU.add), waits=[t1b], tag=f"ms2{k}.{ts}", small=True)
            t2 = sch.step("vector", lambda e, s=ss: e.tensor_tensor(
                out=s[:, 9:10], in0=s[:, 21:22], in1=s[:, 20:21],
                op=ALU.subtract), waits=[t2b, t2c], tag=f"var{k}.{ts}", small=True)
            t3 = sch.step("vector", lambda e, s=ss: e.tensor_scalar(
                out=s[:, 10:11], in0=s[:, 9:10], scalar1=1.0, scalar2=EPS,
                op0=ALU.mult, op1=ALU.add), waits=[t2], tag=f"vpe{k}.{ts}", small=True)
            t4 = sch.step("scalar", lambda e, s=ss: e.activation(
                out=s[:, 11:12], in_=s[:, 10:11], func=AF.Ln), waits=[t3],
                tag=f"ln1{k}.{ts}", small=True)
            t5 = sch.step("scalar", lambda e, s=ss: e.activation(
                out=s[:, 12:13], in_=s[:, 11:12], func=AF.Exp, scale=-0.5),
                waits=[t4], tag=f"rstd{k}.{ts}", small=True)
            t6 = sch.step("vector", lambda e, s=ss: e.tensor_scalar(
                out=s[:, 13:14], in0=s[:, 9:10], scalar1=1.0 + EPS,
                scalar2=EPS * EPS, op0=ALU.mult, op1=ALU.add), waits=[t2],
                tag=f"vp2{k}.{ts}", small=True)
            t7 = sch.step("scalar", lambda e, s=ss: e.activation(
                out=s[:, 14:15], in_=s[:, 13:14], func=AF.Ln), waits=[t6],
                tag=f"ln2{k}.{ts}", small=True)
            t8 = sch.step("vector", lambda e, s=ss: e.tensor_tensor(
                out=s[:, 15:16], in0=s[:, 11:12], in1=s[:, 14:15],
                op=ALU.subtract), waits=[t7, t4], tag=f"dln{k}.{ts}", small=True)
            rr2tok = sch.step("scalar", lambda e, s=ss: e.activation(
                out=s[:, 16:17], in_=s[:, 15:16], func=AF.Exp, scale=0.5),
                waits=[t8], tag=f"rr2{k}.{ts}", small=True)
            st_s1 = stats[:, par * 16 + ts:par * 16 + ts + 1]
            st_s2 = stats[:, par * 16 + 4 + ts:par * 16 + 4 + ts + 1]
            epi_w = [last["epi"]] if last["epi"] else []
            e1 = sch.step("vector", lambda e, s=ss, d=st_s1: e.tensor_scalar(
                out=d, in0=s[:, 12:13], scalar1=1.0, scalar2=1.0,
                op0=ALU.mult, op1=ALU.add), waits=[t5] + epi_w, tag=f"s1{k}.{ts}", small=True)
            e2a = sch.step("vector", lambda e, s=ss: e.tensor_tensor(
                out=s[:, 17:18], in0=s[:, 8:9], in1=s[:, 12:13], op=ALU.mult),
                waits=[t5], tag=f"ms{k}.{ts}", small=True)
            e2 = sch.step("vector", lambda e, s=ss, d=st_s2: e.tensor_scalar(
                out=d, in0=s[:, 17:18], scalar1=-1.0, scalar2=0.0,
                op0=ALU.mult, op1=ALU.add), waits=[e2a] + epi_w, tag=f"s2{k}.{ts}", small=True)
            xntok = sch.step("scalar", lambda e, x2=xs_, s=ss, d=st_s2:
                             e.activation(out=x2, in_=x2, func=AF.Identity,
                                          scale=s[:, 12:13], bias=d),
                             waits=[t5, e2, e1], tag=f"xn{k}.{ts}")
            pmw = [last["pm_free"][i]] if last["pm_free"][i] else []
            trtok = sch.step("tensor", lambda e, s=ss, i=i: e.transpose(
                out=pm[i][0:1, 256:384], in_=s[:, 16:17], identity=identf),
                waits=[rr2tok, t_idr] + pmw, tag=f"rrT{k}.{ts}")
            rk, rs, rw = rowt.acquire()
            retok = sch.step("scalar", lambda e, r=rs, i=i: e.copy(
                out=r[0:1, 0:128], in_=pm[i][0:1, 256:384]),
                waits=[trtok] + rw, tag=f"rrE{k}.{ts}", small=True)
            rmtok = sch.step("tensor", lambda e, r=rs, i=i: e.matmul(
                out=pm[i][:, 256:384], lhsT=onesr, rhs=r[0:1, 0:128],
                start=True, stop=True, skip_group_check=True),
                waits=[retok, t_onr], tag=f"rrM{k}.{ts}")
            rr2w = [last["rr2_reader"]] if last["rr2_reader"] else []
            rbtok = sch.step("scalar", lambda e, i=i: e.copy(
                out=rr2b[:, i * 128:(i + 1) * 128], in_=pm[i][:, 256:384]),
                waits=[rmtok] + rr2w, tag=f"rrB{k}.{ts}")
            last["pm_free"][i] = rbtok
            rowt.release(rk, rmtok)
            ctx["rr2_ready"][ts] = rbtok
            lasttp = None
            for jb in range(2):
                bank = jb
                wv = [xntok, t_idr]
                if last["pm_free"][bank]:
                    wv.append(last["pm_free"][bank])
                def tp_group(e, x2=xs_, b=bank, jb=jb):
                    mm = None
                    for q in range(4):
                        jj = jb * 4 + q
                        mm = e.transpose(
                            out=pm[b][:, q * 128:(q + 1) * 128],
                            in_=x2[:, jj * 128:(jj + 1) * 128], identity=identf)
                    return mm
                ttok = sch.step("tensor", tp_group, waits=wv,
                                tag=f"tp{k}.{ts}.{jb}")
                ttoks = [ttok]
                lasttp = ttok
                etok = None
                for q in range(4):
                    j = jb * 4 + q
                    xnw = [ttoks[-1]]
                    if last["xn_reader"]:
                        xnw.append(last["xn_reader"])
                    etok = sch.step("scalar", lambda e, jj=j, b=bank, qq=q,
                                    t2=ts: e.copy(
                        out=xnbf[:, jj * T + t2 * 128:jj * T + (t2 + 1) * 128],
                        in_=pm[b][:, qq * 128:(qq + 1) * 128]),
                        waits=xnw, tag=f"xe{k}.{ts}.{j}")
                    ctx["xn_ready"].append(etok)
                last["pm_free"][bank] = etok
            hstat.release(sk, rbtok)
            xr.release(kx, lasttp)

    def emit_dt(k, h, c):
        bank = c % 2
        w = [ctx["deltar_tok"], xpc_tok]
        if last["pm_free"][bank]:
            w.append(last["pm_free"][bank])
        mtok = sch.step("tensor", lambda e, c=c, b=bank: e.matmul(
            out=pm[b][:, 0:HF], lhsT=dtbf[:, c * 128:(c + 1) * 128], rhs=deltar,
            start=True, stop=True, skip_group_check=True),
            waits=w, tag=f"dt{k}.{h}.{c}")
        sk, ss, sw = scr1.acquire()
        sp = sch.step("scalar", lambda e, b=bank, s=ss, c=c: e.activation(
            out=s[:, 0:HF], in_=pm[b][:, 0:HF], func=AF.Exp,
            bias=dtb[:, c:c + 1]), waits=[mtok] + sw, tag=f"sp{k}.{h}.{c}")
        dk, dslot, dw = delta.acquire()
        dtok = sch.step("scalar", lambda e, s=ss, d=dslot: e.activation(
            out=d, in_=s[:, 0:HF], func=AF.Ln, bias=1.0),
            waits=[sp] + dw, tag=f"dl{k}.{h}.{c}")
        scr1.release(sk, dtok)
        last["pm_free"][bank] = sp
        last["delta_tok"][c] = dtok
        last["delta_slot"][c] = (dk, dslot)

    def emit_mix(k, h):
        rr_ready = [ctx["rr2_ready"][h * 2], ctx["rr2_ready"][h * 2 + 1]]
        xn_ready = list(ctx["xn_ready"])
        for og in range(8):
            kslot, slot, wtok = mats["in"].load(0, KD, og * 256, 256)
            bank = og % 2
            mmw = [wtok] + xn_ready
            if last["pm_free"][bank]:
                mmw.append(last["pm_free"][bank])
            def in_group(e, s=slot, b=bank, h=h):
                mm = None
                for kb in range(KD):
                    for oc2 in range(2):
                        mm = e.matmul(
                            out=pm[b][:, oc2 * HF:(oc2 + 1) * HF],
                            lhsT=s[:, kb * 256 + oc2 * 128:
                                   kb * 256 + (oc2 + 1) * 128],
                            rhs=xnbf[:, kb * T + h * HF:kb * T + (h + 1) * HF],
                            start=(kb == 0 and oc2 == 0), stop=(kb == KD - 1),
                            skip_group_check=True)
                return mm
            mtok = sch.step("tensor", in_group, waits=mmw,
                            tag=f"inmm{k}.{h}.{og}")
            wring.release(kslot, mtok)
            lastev = None
            for oc2 in range(2):
                c = og * 2 + oc2
                pk, pslot, pw = xspre.acquire()
                evtok = sch.step("vector", lambda e, b=bank, oc2=oc2, p=pslot:
                                 e.scalar_tensor_tensor(
                                     out=p[:, 4:4 + HF],
                                     in0=pm[b][:, oc2 * HF:(oc2 + 1) * HF],
                                     scalar=1.0, in1=rr2b,
                                     op0=ALU.mult, op1=ALU.mult),
                                 waits=[mtok] + rr_ready + pw,
                                 tag=f"xsev{k}.{h}.{c}")
                lastev = evtok
                histok = sch.step("vector", lambda e, p=pslot, c=c: e.tensor_copy(
                    out=p[:, 1:4], in_=tails[:, c * 3:(c + 1) * 3]),
                    waits=[evtok, t_z2], tag=f"hist{k}.{h}.{c}", small=True)
                sk, ss, sw = scr1.acquire()
                cv = sch.step("scalar", lambda e, p=pslot, s=ss, c=c: e.activation(
                    out=s[:, 0:HF], in_=p[:, 1:1 + HF], func=AF.Identity,
                    scale=convw[:, c * 4:c * 4 + 1]),
                    waits=[histok] + sw, tag=f"cv0.{k}.{h}.{c}")
                for kk in range(1, 4):
                    cv = sch.step("vector", lambda e, p=pslot, s=ss, c=c, kk=kk:
                                  e.scalar_tensor_tensor(
                                      out=s[:, 0:HF], in0=p[:, 1 + kk:1 + kk + HF],
                                      scalar=convw[:, c * 4 + kk:c * 4 + kk + 1],
                                      in1=s[:, 0:HF], op0=ALU.mult, op1=ALU.add),
                                  waits=[cv], tag=f"cv{kk}.{k}.{h}.{c}")
                xsw = [cv]
                if last["xssm_reader"][c]:
                    xsw.append(last["xssm_reader"][c])
                xstok = sch.step("scalar", lambda e, s=ss, c=c: e.activation(
                    out=xssm[:, c * HF:(c + 1) * HF], in_=s[:, 0:HF],
                    func=AF.Silu, bias=convb[:, c:c + 1]),
                    waits=xsw, tag=f"xsm{k}.{h}.{c}")
                scr1.release(sk, xstok)
                bk, bslot, bw = xsbf.acquire()
                cbtok = sch.step("scalar", lambda e, b=bslot, c=c: e.copy(
                    out=b, in_=xssm[:, c * HF:(c + 1) * HF]),
                    waits=[xstok] + bw, tag=f"xbf{k}.{h}.{c}")
                tltok = sch.step("vector", lambda e, p=pslot, c=c: e.tensor_copy(
                    out=tails[:, c * 3:(c + 1) * 3], in_=p[:, 1 + HF:4 + HF]),
                    waits=[evtok], tag=f"tls{k}.{h}.{c}", small=True,
                    strict=True)
                xspre.release(pk, tltok)
                last["xssm_tok"][c] = xstok
                w0 = [cbtok, xpc_tok]
                if c == 0:
                    for b2 in range(2):
                        if last["py_free"][b2]:
                            w0.append(last["py_free"][b2])
                xp1 = sch.step("tensor", lambda e, b=bslot, c=c: e.matmul(
                    out=py[0][0:DTR, 0:HF], lhsT=xpbf[:, c * 96:c * 96 + DTR],
                    rhs=b, start=(c == 0), stop=(c == CD - 1),
                    skip_group_check=True), waits=w0, tag=f"xpa{k}.{h}.{c}")
                xp2 = sch.step("tensor", lambda e, b=bslot, c=c: e.matmul(
                    out=py[1][0:32, 0:HF], lhsT=xpbf[:, c * 96 + DTR:(c + 1) * 96],
                    rhs=b, start=(c == 0), stop=(c == CD - 1),
                    skip_group_check=True), waits=[], tag=f"xpb{k}.{h}.{c}")
                xsbf.release(bk, xp2)
                if c == CD - 1:
                    ctx["xp_done"] = (xp1, xp2)
            last["pm_free"][bank] = lastev
        xp1, xp2 = ctx["xp_done"]
        drtok = sch.step("scalar", lambda e: e.copy(out=deltar,
                                                    in_=py[0][0:DTR, 0:HF]),
                         waits=[xp1], tag=f"dre{k}.{h}")
        bctok = sch.step("scalar", lambda e: e.copy(out=bcsb,
                                                    in_=py[1][0:32, 0:HF]),
                         waits=[xp2], tag=f"bce{k}.{h}")
        last["py_free"][0] = drtok
        last["py_free"][1] = bctok
        ctx["deltar_tok"] = drtok
        ctx["bc_tok"] = bctok
        ctx["gate_tok"] = {}
        emit_dt(k, h, 0)
        emit_dt(k, h, 1)
        emit_gate_og(k, h, 0)

    def emit_gate_og(k, h, g):
        kslot, slot, wtok = mats["in"].load(0, KD, 2048 + g * 256, 256)
        bank = g % 2
        mmw = [wtok]
        if last["pm_free"][bank]:
            mmw.append(last["pm_free"][bank])
        def g_group(e, s=slot, b=bank, h=h):
            mm = None
            for kb in range(KD):
                for oc2 in range(2):
                    mm = e.matmul(
                        out=pm[b][:, oc2 * HF:(oc2 + 1) * HF],
                        lhsT=s[:, kb * 256 + oc2 * 128:
                               kb * 256 + (oc2 + 1) * 128],
                        rhs=xnbf[:, kb * T + h * HF:kb * T + (h + 1) * HF],
                        start=(kb == 0 and oc2 == 0), stop=(kb == KD - 1),
                        skip_group_check=True)
            return mm
        mtok = sch.step("tensor", g_group, waits=mmw, tag=f"gmm{k}.{h}.{g}")
        wring.release(kslot, mtok)
        lastev = None
        for oc2 in range(2):
            c = g * 2 + oc2
            sk, ss, sw = scr1.acquire()
            ev = sch.step("vector", lambda e, b=bank, oc2=oc2, s=ss:
                          e.scalar_tensor_tensor(
                              out=s[:, 0:HF],
                              in0=pm[b][:, oc2 * HF:(oc2 + 1) * HF],
                              scalar=1.0, in1=rr2b, op0=ALU.mult, op1=ALU.mult),
                          waits=[mtok] + [ctx["rr2_ready"][h * 2],
                                          ctx["rr2_ready"][h * 2 + 1]] + sw,
                          tag=f"gev{k}.{h}.{c}")
            gk, gslot, gw = gate.acquire()
            si = sch.step("scalar", lambda e, s=ss, gsl=gslot: e.activation(
                out=gsl, in_=s[:, 0:HF], func=AF.Silu), waits=[ev] + gw,
                tag=f"gsi{k}.{h}.{c}")
            scr1.release(sk, si)
            ctx["gate_tok"][c] = (gk, gslot, si)
            lastev = ev
        last["pm_free"][bank] = lastev

    def emit_bcasts(k, h):
        bw = [ctx["bc_tok"]]
        if last["bcb_reader"]:
            bw.append(last["bcb_reader"])
        readies = []
        for n in range(NST * 2):
            tgt = bbt[n] if n < NST else cbt[n - NST]
            bank = n % 2
            w1 = list(bw) + [t_idr]
            if last["py_free"][bank]:
                w1.append(last["py_free"][bank])
            m1tok = sch.step("tensor", lambda e, n=n, b=bank: e.matmul(
                out=py[b][0:1, 256:256 + HF], lhsT=identr[0:32, n:n + 1],
                rhs=bcsb, start=True, stop=True, skip_group_check=True),
                waits=w1, tag=f"bcr{k}.{h}.{n}")
            rk, rslot, rw = rowt.acquire()
            e1tok = sch.step("scalar", lambda e, b=bank, r=rslot: e.copy(
                out=r, in_=py[b][0:1, 256:256 + HF]), waits=[m1tok] + rw,
                tag=f"bcre{k}.{h}.{n}")
            m2tok = sch.step("tensor", lambda e, b=bank, r=rslot: e.matmul(
                out=py[b][:, 256:256 + HF], lhsT=onesr, rhs=r, start=True,
                stop=True, skip_group_check=True), waits=[e1tok, t_onr],
                tag=f"bcm{k}.{h}.{n}")
            rowt.release(rk, m2tok)
            e2tok = sch.step("scalar", lambda e, b=bank, tg=tgt: e.copy(
                out=tg, in_=py[b][:, 256:256 + HF]), waits=[m2tok],
                tag=f"bcb{k}.{h}.{n}")
            last["py_free"][bank] = e2tok
            readies.append(e2tok)
        ctx["bc_ready"] = readies

    def emit_scan(k, h, pump_items):
        emit_bcasts(k, h)
        bc_ready = ctx["bc_ready"]
        queue = list(pump_items)

        stage_tok = {}
        pv_last = None
        for c in range(CD):
            dk, dslot = last["delta_slot"][c]
            dtok = last["delta_tok"][c]
            uk, uslot, uw = uten.acquire()
            utok = sch.step("vector", lambda e, d=dslot, u=uslot, c=c:
                            e.tensor_tensor(out=u, in0=d,
                                            in1=xssm[:, c * HF:(c + 1) * HF],
                                            op=ALU.mult),
                            waits=[dtok, last["xssm_tok"][c]] + uw,
                            tag=f"u{k}.{h}.{c}")
            last["xssm_reader"][c] = utok
            yk, yslot, yw = ybuf.acquire()
            psk = sch.step("scalar", lambda e, y=yslot, c=c: e.activation(
                out=y, in_=xssm[:, c * HF:(c + 1) * HF], func=AF.Identity,
                scale=dsk[:, c:c + 1]),
                waits=[last["xssm_tok"][c]] + yw, tag=f"psk{k}.{h}.{c}")
            gtok_prev = psk
            sct_last = None
            pv_last = None
            for n in range(NST):
                ak, aslot, aw = dAr.acquire()
                datok = sch.step("scalar", lambda e, d=dslot, a=aslot, c=c, n=n:
                                 e.activation(out=a, in_=d, func=AF.Exp,
                                              scale=ares[:, c * NST + n:
                                                         c * NST + n + 1]),
                                 waits=[dtok, t_a2] + aw,
                                 tag=f"dA{k}.{h}.{c}.{n}")
                bk2, bslot2, bw2 = dBr.acquire()
                dbtok = sch.step("vector", lambda e, u=uslot, b=bslot2, n=n:
                                 e.tensor_tensor(out=b, in0=u, in1=bbt[n],
                                                 op=ALU.mult),
                                 waits=[utok, bc_ready[n]] + bw2,
                                 tag=f"dB{k}.{h}.{c}.{n}")
                sk2, sslot2, sw2 = srng.acquire()
                sctok = sch.step("vector", lambda e, a=aslot, b=bslot2, s=sslot2,
                                 c=c, n=n: e.tensor_tensor_scan(
                    out=s, data0=a, data1=b,
                    initial=state[:, c * NST + n:c * NST + n + 1],
                    op0=ALU.mult, op1=ALU.add),
                    waits=[datok, dbtok, t_z1] + sw2, tag=f"sc{k}.{h}.{c}.{n}")
                dAr.release(ak, sctok)
                dBr.release(bk2, sctok)
                sct_last = sctok
                pk, pslot, pw = pr.acquire()
                ptok = sch.step("vector", lambda e, s=sslot2, p=pslot, n=n:
                                e.tensor_tensor(out=p, in0=s, in1=cbt[n],
                                                op=ALU.mult),
                                waits=[sctok, bc_ready[NST + n]] + pw,
                                tag=f"p{k}.{h}.{c}.{n}")
                sttok = sch.step("vector", lambda e, s=sslot2, c=c, n=n:
                                 e.tensor_copy(
                    out=state[:, c * NST + n:c * NST + n + 1],
                    in_=s[:, HF - 1:HF]), waits=[sctok],
                    tag=f"st{k}.{h}.{c}.{n}", small=True, strict=True)
                srng.release(sk2, sttok)
                pv_last = ptok
                gtok_prev = sch.step("gpsimd", lambda e, y=yslot, p=pslot:
                                     e.tensor_tensor(out=y, in0=y, in1=p,
                                                     op=ALU.add),
                                     waits=[ptok, gtok_prev],
                                     tag=f"yA{k}.{h}.{c}.{n}")
                pr.release(pk, gtok_prev)
            ymm = gtok_prev
            uten.release(uk, pv_last)
            delta.release(dk, sct_last)
            gk, gslot, gtok = ctx["gate_tok"][c]
            ygw = [ymm, gtok]
            if h == 0:
                dst = ygstage[:, c * HF:(c + 1) * HF]
                if last["ycommit"][c]:
                    ygw.append(last["ycommit"][c])
            else:
                dst = ygt[:, c * T + HF:(c + 1) * T]
                if last["ygt_kc"][c]:
                    ygw.append(last["ygt_kc"][c])
            ygtok = sch.step("vector", lambda e, gsl=gslot, y=yslot, d=dst:
                             e.tensor_tensor(out=d, in0=y, in1=gsl,
                                             op=ALU.mult),
                             waits=ygw, tag=f"yg{k}.{h}.{c}")
            gate.release(gk, ygtok)
            ybuf.release(yk, ygtok)
            if h == 0:
                stage_tok[c] = ygtok
            else:
                ctx["ygt_tok"][c] = ygtok
            if c + 2 < CD:
                emit_dt(k, h, c + 2)
            if c % 2 == 1 and (c // 2) + 1 < 8:
                emit_gate_og(k, h, (c // 2) + 1)
            budget = 30.0
            while queue and budget > 0:
                cost, it = queue.pop(0)
                it()
                budget -= cost
        while queue:
            queue.pop(0)[1]()
        if h == 0:
            for c in range(CD):
                cw = [stage_tok[c]]
                if last["ygt_kc"][c]:
                    cw.append(last["ygt_kc"][c])
                ctok = sch.step(
                    "scalar", lambda e, c=c: e.copy(
                        out=ygt[:, c * T:c * T + HF],
                        in_=ygstage[:, c * HF:(c + 1) * HF]),
                    waits=cw, tag=f"ycommit{k}.{c}")
                ctx["ygt_tok"][c] = ctok
                last["ycommit"][c] = ctok
        last["bcb_reader"] = pv_last
        last["rr2_reader"] = pv_last

    def make_ffn(k):
        fc = {"h1in_tok": [], "h2in_tok": [], "h3_tok": [],
              "xn_rd": [], "ffnB_rd": []}

        def w1w2_oc(oc):
            def go():
                k1, s1_, t1 = mats["w1"].load(0, KD, oc * 128, 128)
                k2, s2_, t2 = mats["w2"].load(0, KD, oc * 128, 128)
                bank0, bank1 = (0, 1) if oc % 2 == 0 else (2, 3)
                w1w = [t1]
                if last["pf_free"][bank0]:
                    w1w.append(last["pf_free"][bank0])
                w2w = [t2]
                if last["pf_free"][bank1]:
                    w2w.append(last["pf_free"][bank1])
                def wgrp(e, s, b):
                    mm = None
                    for kb in range(KD):
                        mm = e.matmul(out=pf[b][:, 0:T],
                                      lhsT=s[:, kb * 128:(kb + 1) * 128],
                                      rhs=xnbf[:, kb * T:(kb + 1) * T],
                                      start=(kb == 0), stop=(kb == KD - 1),
                                      skip_group_check=True)
                    return mm
                m1tok = sch.step("tensor", lambda e, s=s1_, b=bank0: wgrp(e, s, b),
                                 waits=w1w, tag=f"w1mm{k}.{oc}")
                m2tok = sch.step("tensor", lambda e, s=s2_, b=bank1: wgrp(e, s, b),
                                 waits=w2w, tag=f"w2mm{k}.{oc}")
                fc["xn_rd"].append(m2tok)
                wring.release(k1, m1tok)
                wring.release(k2, m2tok)
                sk, ss, sw = scr1.acquire()
                s1tok = sch.step("scalar", lambda e, s=ss, b=bank0, oc=oc:
                                 e.activation(out=s, in_=pf[b][:, 0:T],
                                              func=AF.Silu,
                                              scale=s1c[:, oc:oc + 1]),
                                 waits=[m1tok] + sw, tag=f"m1.{k}.{oc}")
                wA = [m2tok, s1tok]
                if last["ffnA_reader"]:
                    wA.append(last["ffnA_reader"])
                htok = sch.step("vector", lambda e, s=ss, b=bank1, oc=oc:
                                e.scalar_tensor_tensor(
                                    out=ffnA[:, oc * T:(oc + 1) * T],
                                    in0=pf[b][:, 0:T], scalar=s2c[:, oc:oc + 1],
                                    in1=s, op0=ALU.mult, op1=ALU.mult),
                                waits=wA, tag=f"h1in{k}.{oc}")
                scr1.release(sk, htok)
                last["pf_free"][bank0] = s1tok
                last["pf_free"][bank1] = htok
                fc["h1in_tok"].append(htok)
            return go

        def hmat_pair(mat, src, dst, scales, oc0, out_list, dep_list, rd_list,
                      wrguard, banks, free_key):
            def go():
                mtoks = []
                for i in range(2):
                    oc = oc0 + i
                    bank = (oc0 // 2 + i) % len(banks) if False else \
                        (oc % len(banks))
                    ka, sa, ta = mats[mat].load(0, 16, oc * 128, 128)
                    w0 = [ta] + list(fc[dep_list])
                    if last[free_key][bank]:
                        w0.append(last[free_key][bank])

                    def hgrp(e, s, b, koff, first, last_):
                        mm = None
                        for kk in range(16):
                            mm = e.matmul(out=banks[b][:, 0:T],
                                          lhsT=s[:, kk * 128:(kk + 1) * 128],
                                          rhs=src[:, (koff + kk) * T:
                                                   (koff + kk + 1) * T],
                                          start=(first and kk == 0),
                                          stop=(last_ and kk == 15),
                                          skip_group_check=True)
                        return mm
                    mtok = sch.step("tensor", lambda e, s=sa, b=bank:
                                    hgrp(e, s, b, 0, True, False),
                                    waits=w0, tag=f"{mat}a{k}.{oc}")
                    wring.release(ka, mtok)
                    kb2, sb2, tb = mats[mat].load(16, 16, oc * 128, 128)
                    mtok = sch.step("tensor", lambda e, s=sb2, b=bank:
                                    hgrp(e, s, b, 16, False, True),
                                    waits=[tb], tag=f"{mat}b{k}.{oc}")
                    wring.release(kb2, mtok)
                    mtoks.append(mtok)
                    if rd_list is not None:
                        fc[rd_list].append(mtok)
                        last["ffnB_reader"] = mtok
                for i in range(2):
                    oc = oc0 + i
                    bank = oc % len(banks)
                    wE = [mtoks[i]]
                    if wrguard is not None and last[wrguard]:
                        wE.append(last[wrguard])
                    etok = sch.step("scalar", lambda e, b=bank, oc=oc:
                                    e.activation(
                        out=dst[:, oc * T:(oc + 1) * T], in_=banks[b][:, 0:T],
                        func=AF.Gelu, scale=scales[:, oc:oc + 1]),
                        waits=wE, tag=f"{mat}e{k}.{oc}")
                    last[free_key][bank] = etok
                    fc[out_list].append(etok)
            return go

        w12h1 = [w1w2_oc(oc) for oc in range(KH)]
        w12h1 += [hmat_pair("h1", ffnA, ffnB, h1sc, oc0, "h2in_tok", "h1in_tok",
                            None, "ffnB_reader", pf, "pf_free")
                  for oc0 in range(0, KH, 2)]
        h2s = [hmat_pair("h2", ffnB, ffnA, h2sc, oc0, "h3_tok", "h2in_tok",
                         "ffnB_rd", "ffnA_reader", py, "py_free")
               for oc0 in range(0, KH, 2)]
        return fc, w12h1, h2s

    def make_tail(k, tctx):
        par = k % 2
        fc = tctx["fc"]
        items = []

        def oh_items(oh):
            res = []
            mms = []

            def go_out(kcg):
                def go():
                    kslot, slot, wtok = mats["out"].load(kcg * 4, 4, oh * 512, 512)
                    w0 = [wtok]
                    for i in range(4):
                        w0 += list(tctx["ygt_tok"][kcg * 4 + i])
                    if kcg == 0:
                        for ts in range(4):
                            if last["pf_free"][ts]:
                                w0.append(last["pf_free"][ts])

                    def ogrp(e, s=slot, kcg=kcg):
                        mm = None
                        for kc4 in range(4):
                            kc = kcg * 4 + kc4
                            for ts in range(4):
                                mm = e.matmul(
                                    out=pf[ts][:, 0:512],
                                    lhsT=ygt[:, kc * T + ts * 128:
                                             kc * T + (ts + 1) * 128],
                                    rhs=s[:, kc4 * 512:(kc4 + 1) * 512],
                                    start=(kc == 0), stop=False,
                                    skip_group_check=True)
                        return mm
                    mtok = sch.step("tensor", ogrp, waits=w0,
                                    tag=f"op{k}.{oh}.{kcg}")
                    if oh == 1:
                        for i in range(4):
                            last["ygt_kc"][kcg * 4 + i] = mtok
                    wring.release(kslot, mtok)
                    mms.append(mtok)
                return go

            def go_w3(kcg):
                def go():
                    kslot, slot, wtok = mats["w3"].load(kcg * 4, 4, oh * 512, 512)
                    w0 = [wtok] + [fc["h3_tok"][kcg * 4 + i] for i in range(4)]

                    def wgrp3(e, s=slot, kcg=kcg):
                        mm = None
                        for kc4 in range(4):
                            kc = kcg * 4 + kc4
                            for ts in range(4):
                                mm = e.matmul(
                                    out=pf[ts][:, 0:512],
                                    lhsT=ffnA[:, kc * T + ts * 128:
                                              kc * T + (ts + 1) * 128],
                                    rhs=s[:, kc4 * 512:(kc4 + 1) * 512],
                                    start=False, stop=(kcg == 7 and kc4 == 3),
                                    skip_group_check=True)
                        return mm
                    mtok = sch.step("tensor", wgrp3, waits=w0,
                                    tag=f"w3.{k}.{oh}.{kcg}")
                    if oh == 1:
                        last["ffnA_reader"] = mtok
                    wring.release(kslot, mtok)
                    mms.append(mtok)
                return go

            def go_evac():
                def go():
                    for ts in range(4):
                        row0 = k * T + ts * 128
                        ek, eslot, ew = exr.acquire()
                        xtok = sch.step("scalar", lambda e, d=eslot,
                                        a=x_hbm[row0:row0 + 128,
                                                oh * 512:(oh + 1) * 512]:
                                        e.dma_start(out=d, in_=a), waits=ew,
                                        sem=f"EX{ek}", inc=16,
                                        tag=f"ex{k}.{ts}.{oh}")
                        ok_, oslot, ow = outr.acquire()
                        v1 = sch.step("vector", lambda e, x2=eslot, o=oslot,
                                      ts=ts: e.scalar_tensor_tensor(
                            out=o, in0=x2,
                            scalar=stats[:, par * 16 + ts:par * 16 + ts + 1],
                            in1=pf[ts][:, 0:512], op0=ALU.mult, op1=ALU.add),
                            waits=[xtok] + list(mms) + ow,
                            tag=f"ev1.{k}.{ts}.{oh}")
                        exr.release(ek, v1)
                        last["pf_free"][ts] = v1
                        v2 = sch.step("scalar", lambda e, o=oslot, ts=ts:
                                      e.activation(
                            out=o, in_=o, func=AF.Identity,
                            bias=stats[:, par * 16 + 4 + ts:
                                       par * 16 + 4 + ts + 1]),
                            waits=[v1], tag=f"ev2.{k}.{ts}.{oh}")
                        otok = dma(out_hbm[row0:row0 + 128,
                                           oh * 512:(oh + 1) * 512],
                                   oslot, waits=[v2], sem=f"DO{ok_}",
                                   tag=f"o{k}.{ts}.{oh}")
                        outr.release(ok_, otok)
                        tctx["out_toks"].append(otok)
                    mms.clear()
                return go

            res = [go_out(g) for g in range(4)] + [go_w3(g) for g in range(8)]
            res.append(go_evac())
            return res

        items += oh_items(0)
        items += oh_items(1)
        return items

    def weave(h2p, tailq):
        h2p = [(14.0, f) for f in h2p]
        tailq = [(5.0, f) for f in tailq]
        q = []
        hi = 0
        for j in range(4):
            q += [h2p[hi], h2p[hi + 1], tailq[j]]
            hi += 2
        for j in range(8):
            if hi < len(h2p):
                q += [h2p[hi], h2p[hi + 1]]
                hi += 2
            q.append(tailq[4 + j])
        q += h2p[hi:]
        q += tailq[12:]
        return q

    prev_tctx = None
    pending_h2 = []
    for k in range(nchunks):
        ctx = {"ygt_tok": {}}
        emit_head(k, 0)
        emit_mix(k, 0)
        pump = []
        if prev_tctx is not None:
            pump = weave(pending_h2, make_tail(k - 1, prev_tctx))
        elif pending_h2:
            pump = [(14.0, f) for f in pending_h2]
        pending_h2 = []
        emit_scan(k, 0, pump_items=pump)
        if prev_tctx is not None:
            last["epi"] = prev_tctx["out_toks"][-1]
        commitsA = dict(ctx["ygt_tok"])
        ctx = {"ygt_tok": {}}
        emit_head(k, 1)
        emit_mix(k, 1)
        fc, w12h1, h2items = make_ffn(k)
        costed = [(3.5, f) for f in w12h1[:KH]] + [(14.0, f) for f in w12h1[KH:]]
        emit_scan(k, 1, pump_items=costed)
        pending_h2 = h2items
        last["xn_reader"] = fc["xn_rd"][-1]
        tctx = {"ygt_tok": {c: [commitsA[c], ctx["ygt_tok"][c]]
                            for c in range(CD)},
                "fc": fc, "out_toks": []}
        prev_tctx = tctx

    for it in pending_h2:
        it()
    for it in make_tail(nchunks - 1, prev_tctx):
        it()

    tapspec = os.environ.get("MKTAPS", "")
    if tapspec:
        allt = dict(xnbf=xnbf, xssm=xssm, ygt=ygt, ffnA=ffnA, ffnB=ffnB,
                    state=state, stats=stats, rr2b=rr2b, ares=ares,
                    deltar=deltar, bcsb=bcsb, w3sb=w3sb, xpbf=xpbf, dtbf=dtbf,
                    convw=convw, normw=normw, tails=tails)
        for n_ in range(NST):
            allt[f"bb{n_}"] = bbt[n_]
            allt[f"cb{n_}"] = cbt[n_]
        eng_fin = [(s, sch.total[s]) for s in ("T", "V", "S") if s in sch.total]
        for tn in tapspec.split(","):
            ap = allt[tn]
            dt_ = ap.tensor.dtype
            tout = nc.dram_tensor("tap_" + tn, list(ap.shape), dt_,
                                  kind="ExternalOutput").ap()
            sch.step("sync", lambda e, a=ap, o=tout: e.dma_start(out=o, in_=a),
                     waits=eng_fin, sem="DO0", tag=f"tap{tn}")
    eng_sems = {"T", "V", "S", "G"}
    fin_waits = [(s, sch.total[s]) for s in sorted(sch.total)
                 if s not in eng_sems]
    sch.step("sync", lambda e: e.nop(), waits=fin_waits, sem="DW", tag="fin")

    with nc.Block() as block:
        sch.emit(nc, block)
    return nc


def _prep(inputs):
    f = np.float32
    x = np.ascontiguousarray(np.asarray(inputs["L0_attn_x"], dtype=f))
    w_in_t = np.ascontiguousarray(np.asarray(inputs["L0_attn_in_proj_w"], f).T)
    w_out_t = np.ascontiguousarray(np.asarray(inputs["L0_attn_out_proj_w"], f).T)
    xproj = np.asarray(inputs["L0_attn_x_proj_w"], f)
    w_xp_fm = np.zeros((128, CD * 96), dtype=f)
    for c in range(CD):
        w_xp_fm[:, c * 96:(c + 1) * 96] = xproj[:, c * 128:(c + 1) * 128].T
    w_dt_t = np.ascontiguousarray(np.asarray(inputs["L0_attn_dt_proj_w"], f).T)
    w1_t = np.ascontiguousarray(np.asarray(inputs["L0_ffn_w1_w"], f).T)
    w2_t = np.ascontiguousarray(np.asarray(inputs["L0_ffn_w2_w"], f).T)
    h1_t = np.ascontiguousarray(np.asarray(inputs["L0_ffn_h1_w"], f).T)
    h2_t = np.ascontiguousarray(np.asarray(inputs["L0_ffn_h2_w"], f).T)
    w3_t = np.ascontiguousarray(np.asarray(inputs["L0_ffn_w3_w"], f).T)

    def fmcols(v, ncol):
        return np.ascontiguousarray(np.asarray(v, f).reshape(ncol, 128).T)

    conv = np.asarray(inputs["L0_attn_conv_w"], f).reshape(DI, 4)
    conv_fm = np.zeros((128, CD * 4), dtype=f)
    for c in range(CD):
        conv_fm[:, c * 4:(c + 1) * 4] = conv[c * 128:(c + 1) * 128, :]
    alog = np.asarray(inputs["L0_attn_A_log"], f)
    alog_fm = np.zeros((128, CD * NST), dtype=f)
    for c in range(CD):
        alog_fm[:, c * NST:(c + 1) * NST] = alog[c * 128:(c + 1) * 128, :]

    common = dict(
        w_in_t=w_in_t, w_out_t=w_out_t, w_xp_fm=w_xp_fm, w_dt_t=w_dt_t,
        w1_t=w1_t, w2_t=w2_t, h1_t=h1_t, h2_t=h2_t, w3_t=w3_t,
        conv_fm=conv_fm,
        convb_fm=fmcols(inputs["L0_attn_conv_b"], CD),
        dtb_fm=fmcols(inputs["L0_attn_dt_proj_b"], CD),
        dsk_fm=fmcols(inputs["L0_attn_D_skip"], CD),
        alog_fm=alog_fm,
        norm_fm=fmcols(inputs["L0_attn_norm_w"], KD),
        s1_fm=fmcols(inputs["L0_ffn_w1_s"], KH),
        s2_fm=fmcols(inputs["L0_ffn_w2_s"], KH),
        h1s_fm=fmcols(inputs["L0_ffn_h1_s"], KH),
        h2s_fm=fmcols(inputs["L0_ffn_h2_s"], KH),
        w3s_row=np.ascontiguousarray(np.asarray(inputs["L0_ffn_w3_s"], f)
                                     .reshape(1, D)),
        ident_in=np.eye(128, dtype=f),
        ones_in=np.ones((1, 128), dtype=f),
    )
    in_maps = []
    for b in range(8):
        m = dict(common)
        m["x"] = np.ascontiguousarray(x[b])
        in_maps.append(m)
    return in_maps


def kernel(**inputs):
    in_maps = _prep(inputs)
    nc = build()
    res = run_bass_kernel_spmd(nc, in_maps, core_ids=list(range(8)))
    out = np.stack([np.asarray(r["out"], dtype=np.float32)
                    for r in res.results], axis=0)
    return out


# version 14 (final)
# speedup vs baseline: 1.1430x; 1.1362x over previous
import os
import numpy as np
import concourse.bass as bass
import concourse.mybir as mybir
from concourse.bass_utils import run_bass_kernel_spmd

F32 = mybir.dt.float32
F32R = mybir.dt.float32r
BF16 = mybir.dt.bfloat16
AF = mybir.ActivationFunctionType
ALU = mybir.AluOpType

D = 1024
DI = 2048
L = 4096
NCH = 8
T = 512
HF = 256
KD = 8
CD = 16
NST = 16
DTR = 64
HID = 4096
KH = 32
EPS = 1e-5


class Sched:
    def __init__(self):
        self.steps = []
        self.total = {}
        self.sem_names = set()
        self.small_toks = set()

    def step(self, engine, fn, waits=(), sem=None, inc=None, tag="",
             small=False, strict=False):
        if sem is None:
            sem = {"sync": "DW", "tensor": "T", "vector": "V",
                   "scalar": "S", "gpsimd": "G"}[engine]
        if inc is None:
            inc = 16 if engine == "sync" else 1
        for w in waits:
            assert w is not None, f"None wait in {tag}"
            (s, v) = w
            assert s in self.total and v <= self.total[s], \
                f"wait on future token {s}={v} (now {self.total.get(s)}) tag={tag}"
        self.total[sem] = self.total.get(sem, 0) + inc
        self.sem_names.add(sem)
        self.steps.append((engine, fn, sem, inc, tuple(waits), tag, strict))
        tok = (sem, self.total[sem])
        if small:
            self.small_toks.add(tok)
        return tok

    def emit(self, nc, block):
        sems = {name: nc.alloc_semaphore("sem_" + name)
                for name in sorted(self.sem_names)}
        own = {"sync": None, "tensor": "T", "vector": "V", "scalar": "S",
               "gpsimd": "G"}

        def run(eng_name, eng):
            waited = {}
            for (engine, fn, sem, inc, waits, tag, strict) in self.steps:
                if engine != eng_name:
                    continue
                for (s, v) in waits:
                    if (s == own[eng_name] and not STRICT and not strict
                            and (s, v) not in self.small_toks):
                        continue
                    if waited.get(s, 0) >= v:
                        continue
                    eng.wait_ge(sems[s], v)
                    waited[s] = v
                last = fn(eng)
                last.then_inc(sems[sem], inc)

        @block.sync
        def _(eng):
            run("sync", eng)

        @block.tensor
        def _(eng):
            run("tensor", eng)

        @block.vector
        def _(eng):
            run("vector", eng)

        @block.scalar
        def _(eng):
            run("scalar", eng)

        if any(s[0] == "gpsimd" for s in self.steps):
            @block.gpsimd
            def _(eng):
                run("gpsimd", eng)


class Ring:
    def __init__(self, aps):
        self.aps = list(aps)
        self.free_tok = [None] * len(self.aps)
        self.i = 0

    def acquire(self):
        k = self.i % len(self.aps)
        self.i += 1
        tok = self.free_tok[k]
        return k, self.aps[k], ([tok] if tok else [])

    def release(self, k, token):
        self.free_tok[k] = token


STRICT = os.environ.get("MKRC", "0") == "1"


def build(nchunks=NCH):
    nc = bass.Bass("TRN2", target_bir_lowering=False,
                   detect_race_conditions=STRICT)

    def din(name, shape, dtype=F32):
        return nc.dram_tensor(name, shape, dtype, kind="ExternalInput").ap()

    x_hbm = din("x", [L, D])
    w_in_t = din("w_in_t", [D, 2 * DI])
    w_out_t = din("w_out_t", [DI, D])
    w_xp_fm = din("w_xp_fm", [128, CD * 96])
    w_dt_t = din("w_dt_t", [DTR, DI])
    w1_t = din("w1_t", [D, HID])
    w2_t = din("w2_t", [D, HID])
    h1_t = din("h1_t", [HID, HID])
    h2_t = din("h2_t", [HID, HID])
    w3_t = din("w3_t", [HID, D])
    conv_fm = din("conv_fm", [128, CD * 4])
    convb_fm = din("convb_fm", [128, CD])
    dtb_fm = din("dtb_fm", [128, CD])
    dsk_fm = din("dsk_fm", [128, CD])
    alog_fm = din("alog_fm", [128, CD * NST])
    norm_fm = din("norm_fm", [128, KD])
    s1_fm = din("s1_fm", [128, KH])
    s2_fm = din("s2_fm", [128, KH])
    h1s_fm = din("h1s_fm", [128, KH])
    h2s_fm = din("h2s_fm", [128, KH])
    w3s_row = din("w3s_row", [1, D])
    ident_in = din("ident_in", [128, 128])
    ones_in = din("ones_in", [1, 128])

    out_hbm = nc.dram_tensor("out", [L, D], F32, kind="ExternalOutput").ap()

    scr = {}
    for nm, ntiles, tcols in [
        ("in", 16, 2048), ("out", 8, 2048), ("w1", 32, 1024), ("w2", 32, 1024),
        ("h1", 64, 2048), ("h2", 64, 2048), ("w3", 16, 2048),
    ]:
        scr[nm] = nc.dram_tensor("scr_" + nm, [ntiles * 128, tcols], BF16,
                                 kind="Internal").ap()

    def sb(name, shape, dtype=F32):
        return nc.alloc_sbuf_tensor(name, shape, dtype).ap()

    xr = Ring([sb(f"xr{i}", [128, 1024]) for i in range(1)])
    exr = Ring([sb(f"exr{i}", [128, 512]) for i in range(1)])
    stats = sb("stats", [128, 32])
    xnbf = sb("xnbf", [128, KD * T], BF16)
    xssm = sb("xssm", [128, CD * HF])
    xspre = Ring([sb(f"xspre{i}", [128, HF + 4]) for i in range(2)])
    xsbf = Ring([sb(f"xsbf{i}", [128, HF], BF16) for i in range(1)])
    delta = Ring([sb(f"delta{i}", [128, HF]) for i in range(2)])
    uten = Ring([sb("uten0", [128, HF])])
    dAr = Ring([sb(f"dA{i}", [128, HF]) for i in range(2)])
    dBr = Ring([sb(f"dB{i}", [128, HF]) for i in range(1)])
    srng = Ring([sb(f"sr{i}", [128, HF]) for i in range(1)])
    pr = Ring([sb(f"pr{i}", [128, HF]) for i in range(4)])
    ybuf = Ring([sb(f"ybuf{i}", [128, HF]) for i in range(2)])
    bbt = [sb(f"bb{n}", [128, HF]) for n in range(NST)]
    cbt = [sb(f"cb{n}", [128, HF]) for n in range(NST)]
    gate = Ring([sb(f"gate{i}", [128, HF]) for i in range(2)])
    ygt = sb("ygt", [128, CD * T], BF16)
    ygstage = sb("ygstage", [128, CD * HF], BF16)
    ffnA = sb("ffnA", [128, KH * T], BF16)
    ffnB = sb("ffnB", [128, KH * T], BF16)
    scr1 = Ring([sb(f"scr1_{i}", [128, 512]) for i in range(2)])
    fring = Ring([sb(f"fr{i}", [128, 512]) for i in range(1)])
    hstat = Ring([sb(f"hstat{i}", [128, 24]) for i in range(2)])
    outr = Ring([sb(f"outr{i}", [128, 512]) for i in range(1)])
    wring = Ring([sb(f"wr{i}", [128, 2048], BF16) for i in range(3)])
    state = sb("state", [128, CD * NST])
    xpbf = sb("xpbf", [128, CD * 96], BF16)
    dtbf = sb("dtbf", [DTR, DI], BF16)
    deltar = sb("deltar", [DTR, HF], BF16)
    bcsb = sb("bcsb", [32, HF], F32R)
    ares = sb("ares", [128, CD * NST])
    convw = sb("convw", [128, CD * 4])
    convb = sb("convb", [128, CD])
    dtb = sb("dtb", [128, CD])
    dsk = sb("dsk", [128, CD])
    normw = sb("normw", [128, KD])
    s1c = sb("s1c", [128, KH])
    s2c = sb("s2c", [128, KH])
    h1sc = sb("h1sc", [128, KH])
    h2sc = sb("h2sc", [128, KH])
    w3sb = sb("w3sb", [128, D])
    identf = sb("identf", [128, 128])
    identr = sb("identr", [128, 128], F32R)
    onesr = sb("onesr", [1, 128], F32R)
    rowt = Ring([sb(f"rowt{i}", [1, HF], F32R) for i in range(1)])
    rr2b = sb("rr2b", [128, HF])
    lnscr = sb("lnscr", [128, 128])
    tails = sb("tails", [128, CD * 3])

    def psalloc(name):
        return nc.alloc_psum_tensor(name, [128, 512], F32).ap()

    pm = [psalloc("pm0"), psalloc("pm1")]
    pf = [psalloc(f"pf{i}") for i in range(4)]
    py = [psalloc("py0"), psalloc("py1")]

    sch = Sched()

    def dma(dst, src, waits=(), tag="", sem="DW"):
        return sch.step("sync", lambda e: e.dma_start(out=dst, in_=src),
                        waits=waits, sem=sem, tag=tag)

    for dst, src_ in [
        (convw, conv_fm), (convb, convb_fm), (dtb, dtb_fm), (dsk, dsk_fm),
        (ares, alog_fm), (normw, norm_fm), (s1c, s1_fm), (s2c, s2_fm),
        (h1sc, h1s_fm), (identf, ident_in), (h2sc, h2s_fm),
        (w3sb[0:1, :], w3s_row),
    ]:
        dma(dst, src_, tag="p:load", sem="DW")
    ptok_all = ("DW", sch.total["DW"])
    ptoks = [ptok_all]
    fk0, fs0, fw0 = fring.acquire()
    t_of = dma(fs0[0:1, 0:128], ones_in, waits=fw0, sem="FR0", tag="p:onesf")
    t_onr2 = sch.step("scalar", lambda e, s=fs0: e.copy(out=onesr,
                                                        in_=s[0:1, 0:128]),
                      waits=[t_of], tag="p:onr2")
    fring.release(fk0, t_onr2)
    t_a1 = sch.step("scalar", lambda e: e.activation(out=ares, in_=ares, func=AF.Exp),
                    waits=ptoks, tag="p:expA")
    t_a2 = sch.step("scalar", lambda e: e.activation(out=ares, in_=ares,
                                                     func=AF.Identity, scale=-1.0),
                    waits=[t_a1], tag="p:negA")
    t_idr = sch.step("scalar", lambda e: e.copy(out=identr, in_=identf),
                     waits=ptoks, tag="p:idr")
    t_onr = t_onr2
    w3sb_tok = None
    last_pm_seed = [None, None]
    for j in range(4):
        rk, rs, rw = rowt.acquire()
        tc = sch.step("scalar", lambda e, r=rs, j=j: e.copy(
            out=r, in_=w3sb[0:1, j * 256:(j + 1) * 256]),
            waits=ptoks + rw + ([w3sb_tok] if w3sb_tok else []), tag=f"p:w3r{j}")
        tm = sch.step("tensor", lambda e, r=rs, j=j: e.matmul(
            out=pm[j % 2][:, 0:256], lhsT=onesr, rhs=r, start=True, stop=True,
            skip_group_check=True), waits=[tc, t_onr], tag=f"p:w3m{j}")
        rowt.release(rk, tm)
        w3sb_tok = sch.step("scalar", lambda e, j=j: e.copy(
            out=w3sb[:, j * 256:(j + 1) * 256], in_=pm[j % 2][:, 0:256]),
            waits=[tm], tag=f"p:w3e{j}")
        last_pm_seed[j % 2] = w3sb_tok

    xp_toks = []
    for j in range(3):
        fk, fs, fw = fring.acquire()
        dtok = dma(fs[:, 0:512], w_xp_fm[:, j * 512:(j + 1) * 512],
                   waits=fw + [ptok_all], sem=f"FR{fk}", tag=f"p:xpf{j}")
        ctok = sch.step("scalar", lambda e, s=fs, jj=j: e.copy(
            out=xpbf[:, jj * 512:(jj + 1) * 512], in_=s[:, 0:512]),
            waits=[dtok], tag=f"p:xpc{j}")
        fring.release(fk, ctok)
        xp_toks.append(ctok)
    for j in range(4):
        fk, fs, fw = fring.acquire()
        dtok = dma(fs[0:DTR, 0:512], w_dt_t[:, j * 512:(j + 1) * 512],
                   waits=fw, sem=f"FR{fk}", tag=f"p:dtf{j}")
        ctok = sch.step("scalar", lambda e, s=fs, jj=j: e.copy(
            out=dtbf[:, jj * 512:(jj + 1) * 512], in_=s[0:DTR, 0:512]),
            waits=[dtok], tag=f"p:dtc{j}")
        fring.release(fk, ctok)
        xp_toks.append(ctok)
    t_z1 = sch.step("vector", lambda e: e.memset(state, 0.0), tag="p:zst")
    t_z2 = sch.step("vector", lambda e: e.memset(tails, 0.0), tag="p:ztl")
    xpc_tok = xp_toks[-1]

    class WMat:

        def __init__(self, name, src, scratch, scale_cols=None, w3_scale=False):
            self.name = name
            self.src = src
            self.scratch = scratch
            self.scale_cols = scale_cols
            self.w3_scale = w3_scale
            self.seen = {}
            self.next_slot = 0

        def convert(self, k0, nk, o0, no):
            key = (k0, nk, o0, no)
            assert key not in self.seen
            ti = self.next_slot
            self.next_slot += 1
            bk, bslot, bw = cvb.acquire()
            cast_toks = []
            nb = max(1, 512 // no)
            kb = 0
            while kb < nk:
                g = min(nb, nk - kb)
                src3 = self.src[(k0 + kb) * 128:(k0 + kb + g) * 128, o0:o0 + no]
                fk, fs, fw = cvf.acquire()
                if g > 1:
                    src_ap = src3.rearrange("(b p) o -> p b o", p=128)
                    dst_ap = fs.rearrange("p (b o) -> p b o", b=g)
                else:
                    src_ap = src3
                    dst_ap = fs[:, 0:no]
                dtok = sch.step("sync", lambda e, d=dst_ap, a=src_ap:
                                e.dma_start(out=d, in_=a),
                                waits=fw + bw, sem=f"CF{fk}",
                                tag=f"{self.name}cf{key}.{kb}")
                bw = []
                tgt = bslot[:, kb * no:(kb + g) * no]
                if self.scale_cols is not None:
                    ct = dtok
                    for b2 in range(g):
                        kk = k0 + kb + b2
                        ct = sch.step("gpsimd", lambda e, s=fs, b2=b2, kk=kk,
                                      kb=kb, no=no, bs=bslot:
                                      e.tensor_scalar(
                            out=bs[:, (kb + b2) * no:(kb + b2 + 1) * no],
                            in0=s[:, b2 * no:(b2 + 1) * no],
                            scalar1=self.scale_cols[:, kk:kk + 1], scalar2=0.0,
                            op0=ALU.mult, op1=ALU.add),
                            waits=[dtok], tag=f"{self.name}cv{key}.{kb}.{b2}")
                    ctok = ct
                elif self.w3_scale:
                    ctok = sch.step("gpsimd", lambda e, s=fs, t=tgt, g=g, no=no:
                                    e.tensor_tensor(
                        out=t, in0=s[:, 0:g * no],
                        in1=w3sb[:, o0:o0 + no], op=ALU.mult),
                        waits=[dtok, w3sb_tok], tag=f"{self.name}cv{key}.{kb}")
                else:
                    ctok = sch.step("gpsimd", lambda e, s=fs, t=tgt, g=g, no=no:
                                    e.tensor_copy(out=t, in_=s[:, 0:g * no]),
                                    waits=[dtok], tag=f"{self.name}cv{key}.{kb}")
                cvf.release(fk, ctok)
                cast_toks.append(ctok)
                kb += g
            wbt = sch.step("scalar", lambda e, s=bslot, nn=nk * no, t=ti:
                           e.dma_start(out=self.scratch[t * 128:(t + 1) * 128,
                                                        0:nn],
                                       in_=s[:, 0:nn]),
                           waits=cast_toks, sem=f"CB{bk}", inc=16,
                           tag=f"{self.name}cwb{key}")
            cvb.release(bk, wbt)
            self.seen[key] = (ti, wbt)

        def load(self, k0, nk, o0, no, waits=()):
            key = (k0, nk, o0, no)
            kslot, slot, aw = wring.acquire()
            ti, wbt = self.seen[key]
            src = self.scratch[ti * 128:(ti + 1) * 128, 0:nk * no]
            tok = sch.step("sync", lambda e, s=slot, a=src, nn=nk * no:
                           e.dma_start(out=s[:, 0:nn], in_=a),
                           waits=list(waits) + aw + [wbt],
                           sem=f"WR{kslot}", tag=f"{self.name}ld{key}")
            return kslot, slot, tok

    mats = {
        "in": WMat("in", w_in_t, scr["in"], scale_cols=normw),
        "out": WMat("out", w_out_t, scr["out"]),
        "w1": WMat("w1", w1_t, scr["w1"]),
        "w2": WMat("w2", w2_t, scr["w2"]),
        "h1": WMat("h1", h1_t, scr["h1"]),
        "h2": WMat("h2", h2_t, scr["h2"]),
        "w3": WMat("w3", w3_t, scr["w3"], w3_scale=True),
    }

    ffnA_f32 = ffnA.bitcast(F32)
    ffnB_b16 = ffnB
    cvf = Ring([ffnA_f32[:, i * 512:(i + 1) * 512] for i in range(8)])
    cvb = Ring([ffnB_b16[:, i * 2048:(i + 1) * 2048] for i in range(4)])
    for og in range(16):
        mats["in"].convert(0, KD, og * 256, 256)
    for oc in range(KH):
        mats["w1"].convert(0, KD, oc * 128, 128)
        mats["w2"].convert(0, KD, oc * 128, 128)
    for oc in range(KH):
        mats["h1"].convert(0, 16, oc * 128, 128)
        mats["h1"].convert(16, 16, oc * 128, 128)
    for oc in range(KH):
        mats["h2"].convert(0, 16, oc * 128, 128)
        mats["h2"].convert(16, 16, oc * 128, 128)
    for oh in range(2):
        for kcg in range(4):
            mats["out"].convert(kcg * 4, 4, oh * 512, 512)
        for kcg in range(8):
            mats["w3"].convert(kcg * 4, 4, oh * 512, 512)
    conv_toks = [t for t in cvf.free_tok + cvb.free_tok if t]
    conv_done = sch.step("gpsimd", lambda e: e.nop(), waits=conv_toks,
                         tag="p:convdone")
    conv_fa = conv_done
    conv_fb = conv_done

    last = {
        "xn_reader": None,
        "ygt_kc": [None] * CD,
        "ffnA_reader": None,
        "ffnB_reader": None,
        "bcb_reader": None,
        "rr2_reader": None,
        "pm_free": [None, None],
        "pf_free": [None, None, None, None],
        "py_free": [None, None],
        "delta_tok": [None] * CD,
        "delta_slot": [None] * CD,
        "xssm_tok": [None] * CD,
        "xssm_reader": [None] * CD,
        "epi": None,
        "ycommit": [None] * CD,
    }
    last["ffnA_reader"] = conv_fa
    last["ffnB_reader"] = conv_fb
    last["pm_free"] = list(last_pm_seed)

    ctx = {}

    def emit_head(k, h):
        par = k % 2
        ctx["rr2_ready"] = {}
        ctx["xn_ready"] = []
        for i in range(2):
            ts = h * 2 + i
            row0 = k * T + ts * 128
            kx, xs_, xw = xr.acquire()
            xtok = sch.step("scalar", lambda e, d=xs_, a=x_hbm[row0:row0 + 128, :]:
                            e.dma_start(out=d, in_=a), waits=xw,
                            sem=f"XR{kx}", inc=16, tag=f"x{k}.{ts}")
            sk, ss, sw = hstat.acquire()
            t1a = sch.step("scalar", lambda e, s=ss, x2=xs_: e.activation(
                out=x2, in_=x2, func=AF.Identity, accum_out=s[:, 18:19]),
                waits=[xtok] + sw, tag=f"sum{k}.{ts}")
            sq_prev = t1a
            for qi in range(8):
                sq_prev = sch.step("scalar", lambda e, s=ss, x2=xs_, qi=qi:
                                   e.activation(
                    out=lnscr, in_=x2[:, qi * 128:(qi + 1) * 128],
                    func=AF.Square, accum_out=s[:, 19 + (qi > 0) * 3:
                                                20 + (qi > 0) * 3]),
                    waits=[sq_prev], tag=f"sq{qi}.{k}.{ts}")
                if qi > 0:
                    sq_prev = sch.step("vector", lambda e, s=ss:
                                       e.tensor_tensor(
                        out=s[:, 19:20], in0=s[:, 19:20], in1=s[:, 22:23],
                        op=ALU.add), waits=[sq_prev],
                        tag=f"sqa{qi}.{k}.{ts}", small=True)
            t1b = sq_prev
            t2a = sch.step("vector", lambda e, s=ss: e.tensor_scalar(
                out=s[:, 8:9], in0=s[:, 18:19], scalar1=1.0 / D, scalar2=0.0,
                op0=ALU.mult, op1=ALU.add), waits=[t1a], tag=f"mn{k}.{ts}", small=True)
            t2b = sch.step("vector", lambda e, s=ss: e.tensor_tensor(
                out=s[:, 20:21], in0=s[:, 8:9], in1=s[:, 8:9], op=ALU.mult),
                waits=[t2a], tag=f"mn2{k}.{ts}", small=True)
            t2c = sch.step("vector", lambda e, s=ss: e.tensor_scalar(
                out=s[:, 21:22], in0=s[:, 19:20], scalar1=1.0 / D, scalar2=0.0,
                op0=ALU.mult, op1=ALU.add), waits=[t1b], tag=f"ms2{k}.{ts}", small=True)
            t2 = sch.step("vector", lambda e, s=ss: e.tensor_tensor(
                out=s[:, 9:10], in0=s[:, 21:22], in1=s[:, 20:21],
                op=ALU.subtract), waits=[t2b, t2c], tag=f"var{k}.{ts}", small=True)
            t3 = sch.step("vector", lambda e, s=ss: e.tensor_scalar(
                out=s[:, 10:11], in0=s[:, 9:10], scalar1=1.0, scalar2=EPS,
                op0=ALU.mult, op1=ALU.add), waits=[t2], tag=f"vpe{k}.{ts}", small=True)
            t4 = sch.step("scalar", lambda e, s=ss: e.activation(
                out=s[:, 11:12], in_=s[:, 10:11], func=AF.Ln), waits=[t3],
                tag=f"ln1{k}.{ts}", small=True)
            t5 = sch.step("scalar", lambda e, s=ss: e.activation(
                out=s[:, 12:13], in_=s[:, 11:12], func=AF.Exp, scale=-0.5),
                waits=[t4], tag=f"rstd{k}.{ts}", small=True)
            t6 = sch.step("vector", lambda e, s=ss: e.tensor_scalar(
                out=s[:, 13:14], in0=s[:, 9:10], scalar1=1.0 + EPS,
                scalar2=EPS * EPS, op0=ALU.mult, op1=ALU.add), waits=[t2],
                tag=f"vp2{k}.{ts}", small=True)
            t7 = sch.step("scalar", lambda e, s=ss: e.activation(
                out=s[:, 14:15], in_=s[:, 13:14], func=AF.Ln), waits=[t6],
                tag=f"ln2{k}.{ts}", small=True)
            t8 = sch.step("vector", lambda e, s=ss: e.tensor_tensor(
                out=s[:, 15:16], in0=s[:, 11:12], in1=s[:, 14:15],
                op=ALU.subtract), waits=[t7, t4], tag=f"dln{k}.{ts}", small=True)
            rr2tok = sch.step("scalar", lambda e, s=ss: e.activation(
                out=s[:, 16:17], in_=s[:, 15:16], func=AF.Exp, scale=0.5),
                waits=[t8], tag=f"rr2{k}.{ts}", small=True)
            st_s1 = stats[:, par * 16 + ts:par * 16 + ts + 1]
            st_s2 = stats[:, par * 16 + 4 + ts:par * 16 + 4 + ts + 1]
            epi_w = [last["epi"]] if last["epi"] else []
            e1 = sch.step("vector", lambda e, s=ss, d=st_s1: e.tensor_scalar(
                out=d, in0=s[:, 12:13], scalar1=1.0, scalar2=1.0,
                op0=ALU.mult, op1=ALU.add), waits=[t5] + epi_w, tag=f"s1{k}.{ts}", small=True)
            e2a = sch.step("vector", lambda e, s=ss: e.tensor_tensor(
                out=s[:, 17:18], in0=s[:, 8:9], in1=s[:, 12:13], op=ALU.mult),
                waits=[t5], tag=f"ms{k}.{ts}", small=True)
            e2 = sch.step("vector", lambda e, s=ss, d=st_s2: e.tensor_scalar(
                out=d, in0=s[:, 17:18], scalar1=-1.0, scalar2=0.0,
                op0=ALU.mult, op1=ALU.add), waits=[e2a] + epi_w, tag=f"s2{k}.{ts}", small=True)
            xntok = sch.step("scalar", lambda e, x2=xs_, s=ss, d=st_s2:
                             e.activation(out=x2, in_=x2, func=AF.Identity,
                                          scale=s[:, 12:13], bias=d),
                             waits=[t5, e2, e1], tag=f"xn{k}.{ts}")
            pmw = [last["pm_free"][i]] if last["pm_free"][i] else []
            trtok = sch.step("tensor", lambda e, s=ss, i=i: e.transpose(
                out=pm[i][0:1, 256:384], in_=s[:, 16:17], identity=identf),
                waits=[rr2tok, t_idr] + pmw, tag=f"rrT{k}.{ts}")
            rk, rs, rw = rowt.acquire()
            retok = sch.step("scalar", lambda e, r=rs, i=i: e.copy(
                out=r[0:1, 0:128], in_=pm[i][0:1, 256:384]),
                waits=[trtok] + rw, tag=f"rrE{k}.{ts}", small=True)
            rmtok = sch.step("tensor", lambda e, r=rs, i=i: e.matmul(
                out=pm[i][:, 256:384], lhsT=onesr, rhs=r[0:1, 0:128],
                start=True, stop=True, skip_group_check=True),
                waits=[retok, t_onr], tag=f"rrM{k}.{ts}")
            rr2w = [last["rr2_reader"]] if last["rr2_reader"] else []
            rbtok = sch.step("scalar", lambda e, i=i: e.copy(
                out=rr2b[:, i * 128:(i + 1) * 128], in_=pm[i][:, 256:384]),
                waits=[rmtok] + rr2w, tag=f"rrB{k}.{ts}")
            last["pm_free"][i] = rbtok
            rowt.release(rk, rmtok)
            ctx["rr2_ready"][ts] = rbtok
            lasttp = None
            for jb in range(2):
                bank = jb
                wv = [xntok, t_idr]
                if last["pm_free"][bank]:
                    wv.append(last["pm_free"][bank])
                def tp_group(e, x2=xs_, b=bank, jb=jb):
                    mm = None
                    for q in range(4):
                        jj = jb * 4 + q
                        mm = e.transpose(
                            out=pm[b][:, q * 128:(q + 1) * 128],
                            in_=x2[:, jj * 128:(jj + 1) * 128], identity=identf)
                    return mm
                ttok = sch.step("tensor", tp_group, waits=wv,
                                tag=f"tp{k}.{ts}.{jb}")
                ttoks = [ttok]
                lasttp = ttok
                etok = None
                for q in range(4):
                    j = jb * 4 + q
                    xnw = [ttoks[-1]]
                    if last["xn_reader"]:
                        xnw.append(last["xn_reader"])
                    etok = sch.step("scalar", lambda e, jj=j, b=bank, qq=q,
                                    t2=ts: e.copy(
                        out=xnbf[:, jj * T + t2 * 128:jj * T + (t2 + 1) * 128],
                        in_=pm[b][:, qq * 128:(qq + 1) * 128]),
                        waits=xnw, tag=f"xe{k}.{ts}.{j}")
                    ctx["xn_ready"].append(etok)
                last["pm_free"][bank] = etok
            hstat.release(sk, rbtok)
            xr.release(kx, lasttp)

    def emit_dt(k, h, c):
        bank = c % 2
        w = [ctx["deltar_tok"], xpc_tok]
        if last["pm_free"][bank]:
            w.append(last["pm_free"][bank])
        mtok = sch.step("tensor", lambda e, c=c, b=bank: e.matmul(
            out=pm[b][:, 0:HF], lhsT=dtbf[:, c * 128:(c + 1) * 128], rhs=deltar,
            start=True, stop=True, skip_group_check=True),
            waits=w, tag=f"dt{k}.{h}.{c}")
        sk, ss, sw = scr1.acquire()
        sp = sch.step("scalar", lambda e, b=bank, s=ss, c=c: e.activation(
            out=s[:, 0:HF], in_=pm[b][:, 0:HF], func=AF.Exp,
            bias=dtb[:, c:c + 1]), waits=[mtok] + sw, tag=f"sp{k}.{h}.{c}")
        dk, dslot, dw = delta.acquire()
        dtok = sch.step("scalar", lambda e, s=ss, d=dslot: e.activation(
            out=d, in_=s[:, 0:HF], func=AF.Ln, bias=1.0),
            waits=[sp] + dw, tag=f"dl{k}.{h}.{c}")
        scr1.release(sk, dtok)
        last["pm_free"][bank] = sp
        last["delta_tok"][c] = dtok
        last["delta_slot"][c] = (dk, dslot)

    def emit_mix(k, h):
        rr_ready = [ctx["rr2_ready"][h * 2], ctx["rr2_ready"][h * 2 + 1]]
        xn_ready = list(ctx["xn_ready"])
        for og in range(8):
            kslot, slot, wtok = mats["in"].load(0, KD, og * 256, 256)
            bank = og % 2
            mmw = [wtok] + xn_ready
            if last["pm_free"][bank]:
                mmw.append(last["pm_free"][bank])
            def in_group(e, s=slot, b=bank, h=h):
                mm = None
                for kb in range(KD):
                    for oc2 in range(2):
                        mm = e.matmul(
                            out=pm[b][:, oc2 * HF:(oc2 + 1) * HF],
                            lhsT=s[:, kb * 256 + oc2 * 128:
                                   kb * 256 + (oc2 + 1) * 128],
                            rhs=xnbf[:, kb * T + h * HF:kb * T + (h + 1) * HF],
                            start=(kb == 0 and oc2 == 0), stop=(kb == KD - 1),
                            skip_group_check=True)
                return mm
            mtok = sch.step("tensor", in_group, waits=mmw,
                            tag=f"inmm{k}.{h}.{og}")
            wring.release(kslot, mtok)
            lastev = None
            for oc2 in range(2):
                c = og * 2 + oc2
                pk, pslot, pw = xspre.acquire()
                evtok = sch.step("vector", lambda e, b=bank, oc2=oc2, p=pslot:
                                 e.scalar_tensor_tensor(
                                     out=p[:, 4:4 + HF],
                                     in0=pm[b][:, oc2 * HF:(oc2 + 1) * HF],
                                     scalar=1.0, in1=rr2b,
                                     op0=ALU.mult, op1=ALU.mult),
                                 waits=[mtok] + rr_ready + pw,
                                 tag=f"xsev{k}.{h}.{c}")
                lastev = evtok
                histok = sch.step("vector", lambda e, p=pslot, c=c: e.tensor_copy(
                    out=p[:, 1:4], in_=tails[:, c * 3:(c + 1) * 3]),
                    waits=[evtok, t_z2], tag=f"hist{k}.{h}.{c}", small=True)
                sk, ss, sw = scr1.acquire()
                cv = sch.step("scalar", lambda e, p=pslot, s=ss, c=c: e.activation(
                    out=s[:, 0:HF], in_=p[:, 1:1 + HF], func=AF.Identity,
                    scale=convw[:, c * 4:c * 4 + 1]),
                    waits=[histok] + sw, tag=f"cv0.{k}.{h}.{c}")
                for kk in range(1, 4):
                    cv = sch.step("vector", lambda e, p=pslot, s=ss, c=c, kk=kk:
                                  e.scalar_tensor_tensor(
                                      out=s[:, 0:HF], in0=p[:, 1 + kk:1 + kk + HF],
                                      scalar=convw[:, c * 4 + kk:c * 4 + kk + 1],
                                      in1=s[:, 0:HF], op0=ALU.mult, op1=ALU.add),
                                  waits=[cv], tag=f"cv{kk}.{k}.{h}.{c}")
                xsw = [cv]
                if last["xssm_reader"][c]:
                    xsw.append(last["xssm_reader"][c])
                xstok = sch.step("scalar", lambda e, s=ss, c=c: e.activation(
                    out=xssm[:, c * HF:(c + 1) * HF], in_=s[:, 0:HF],
                    func=AF.Silu, bias=convb[:, c:c + 1]),
                    waits=xsw, tag=f"xsm{k}.{h}.{c}")
                scr1.release(sk, xstok)
                bk, bslot, bw = xsbf.acquire()
                cbtok = sch.step("scalar", lambda e, b=bslot, c=c: e.copy(
                    out=b, in_=xssm[:, c * HF:(c + 1) * HF]),
                    waits=[xstok] + bw, tag=f"xbf{k}.{h}.{c}")
                tltok = sch.step("vector", lambda e, p=pslot, c=c: e.tensor_copy(
                    out=tails[:, c * 3:(c + 1) * 3], in_=p[:, 1 + HF:4 + HF]),
                    waits=[evtok], tag=f"tls{k}.{h}.{c}", small=True,
                    strict=True)
                xspre.release(pk, tltok)
                last["xssm_tok"][c] = xstok
                w0 = [cbtok, xpc_tok]
                if c == 0:
                    for b2 in range(2):
                        if last["py_free"][b2]:
                            w0.append(last["py_free"][b2])
                xp1 = sch.step("tensor", lambda e, b=bslot, c=c: e.matmul(
                    out=py[0][0:DTR, 0:HF], lhsT=xpbf[:, c * 96:c * 96 + DTR],
                    rhs=b, start=(c == 0), stop=(c == CD - 1),
                    skip_group_check=True), waits=w0, tag=f"xpa{k}.{h}.{c}")
                xp2 = sch.step("tensor", lambda e, b=bslot, c=c: e.matmul(
                    out=py[1][0:32, 0:HF], lhsT=xpbf[:, c * 96 + DTR:(c + 1) * 96],
                    rhs=b, start=(c == 0), stop=(c == CD - 1),
                    skip_group_check=True), waits=[], tag=f"xpb{k}.{h}.{c}")
                xsbf.release(bk, xp2)
                if c == CD - 1:
                    ctx["xp_done"] = (xp1, xp2)
            last["pm_free"][bank] = lastev
        xp1, xp2 = ctx["xp_done"]
        drtok = sch.step("scalar", lambda e: e.copy(out=deltar,
                                                    in_=py[0][0:DTR, 0:HF]),
                         waits=[xp1], tag=f"dre{k}.{h}")
        bctok = sch.step("scalar", lambda e: e.copy(out=bcsb,
                                                    in_=py[1][0:32, 0:HF]),
                         waits=[xp2], tag=f"bce{k}.{h}")
        last["py_free"][0] = drtok
        last["py_free"][1] = bctok
        ctx["deltar_tok"] = drtok
        ctx["bc_tok"] = bctok
        ctx["gate_tok"] = {}
        emit_dt(k, h, 0)
        emit_dt(k, h, 1)
        emit_gate_og(k, h, 0)

    def emit_gate_og(k, h, g):
        kslot, slot, wtok = mats["in"].load(0, KD, 2048 + g * 256, 256)
        bank = g % 2
        mmw = [wtok]
        if last["pm_free"][bank]:
            mmw.append(last["pm_free"][bank])
        def g_group(e, s=slot, b=bank, h=h):
            mm = None
            for kb in range(KD):
                for oc2 in range(2):
                    mm = e.matmul(
                        out=pm[b][:, oc2 * HF:(oc2 + 1) * HF],
                        lhsT=s[:, kb * 256 + oc2 * 128:
                               kb * 256 + (oc2 + 1) * 128],
                        rhs=xnbf[:, kb * T + h * HF:kb * T + (h + 1) * HF],
                        start=(kb == 0 and oc2 == 0), stop=(kb == KD - 1),
                        skip_group_check=True)
            return mm
        mtok = sch.step("tensor", g_group, waits=mmw, tag=f"gmm{k}.{h}.{g}")
        wring.release(kslot, mtok)
        lastev = None
        for oc2 in range(2):
            c = g * 2 + oc2
            sk, ss, sw = scr1.acquire()
            ev = sch.step("vector", lambda e, b=bank, oc2=oc2, s=ss:
                          e.scalar_tensor_tensor(
                              out=s[:, 0:HF],
                              in0=pm[b][:, oc2 * HF:(oc2 + 1) * HF],
                              scalar=1.0, in1=rr2b, op0=ALU.mult, op1=ALU.mult),
                          waits=[mtok] + [ctx["rr2_ready"][h * 2],
                                          ctx["rr2_ready"][h * 2 + 1]] + sw,
                          tag=f"gev{k}.{h}.{c}")
            gk, gslot, gw = gate.acquire()
            si = sch.step("scalar", lambda e, s=ss, gsl=gslot: e.activation(
                out=gsl, in_=s[:, 0:HF], func=AF.Silu), waits=[ev] + gw,
                tag=f"gsi{k}.{h}.{c}")
            scr1.release(sk, si)
            ctx["gate_tok"][c] = (gk, gslot, si)
            lastev = ev
        last["pm_free"][bank] = lastev

    def emit_bcasts(k, h):
        bw = [ctx["bc_tok"]]
        if last["bcb_reader"]:
            bw.append(last["bcb_reader"])
        readies = []
        for n in range(NST * 2):
            tgt = bbt[n] if n < NST else cbt[n - NST]
            bank = n % 2
            w1 = list(bw) + [t_idr]
            if last["py_free"][bank]:
                w1.append(last["py_free"][bank])
            m1tok = sch.step("tensor", lambda e, n=n, b=bank: e.matmul(
                out=py[b][0:1, 256:256 + HF], lhsT=identr[0:32, n:n + 1],
                rhs=bcsb, start=True, stop=True, skip_group_check=True),
                waits=w1, tag=f"bcr{k}.{h}.{n}")
            rk, rslot, rw = rowt.acquire()
            e1tok = sch.step("scalar", lambda e, b=bank, r=rslot: e.copy(
                out=r, in_=py[b][0:1, 256:256 + HF]), waits=[m1tok] + rw,
                tag=f"bcre{k}.{h}.{n}")
            m2tok = sch.step("tensor", lambda e, b=bank, r=rslot: e.matmul(
                out=py[b][:, 256:256 + HF], lhsT=onesr, rhs=r, start=True,
                stop=True, skip_group_check=True), waits=[e1tok, t_onr],
                tag=f"bcm{k}.{h}.{n}")
            rowt.release(rk, m2tok)
            e2tok = sch.step("scalar", lambda e, b=bank, tg=tgt: e.copy(
                out=tg, in_=py[b][:, 256:256 + HF]), waits=[m2tok],
                tag=f"bcb{k}.{h}.{n}")
            last["py_free"][bank] = e2tok
            readies.append(e2tok)
        ctx["bc_ready"] = readies

    def emit_scan(k, h, pump_items):
        emit_bcasts(k, h)
        bc_ready = ctx["bc_ready"]
        queue = list(pump_items)

        stage_tok = {}
        pv_last = None
        for c in range(CD):
            dk, dslot = last["delta_slot"][c]
            dtok = last["delta_tok"][c]
            uk, uslot, uw = uten.acquire()
            utok = sch.step("vector", lambda e, d=dslot, u=uslot, c=c:
                            e.tensor_tensor(out=u, in0=d,
                                            in1=xssm[:, c * HF:(c + 1) * HF],
                                            op=ALU.mult),
                            waits=[dtok, last["xssm_tok"][c]] + uw,
                            tag=f"u{k}.{h}.{c}")
            last["xssm_reader"][c] = utok
            yk, yslot, yw = ybuf.acquire()
            psk = sch.step("scalar", lambda e, y=yslot, c=c: e.activation(
                out=y, in_=xssm[:, c * HF:(c + 1) * HF], func=AF.Identity,
                scale=dsk[:, c:c + 1]),
                waits=[last["xssm_tok"][c]] + yw, tag=f"psk{k}.{h}.{c}")
            gtok_prev = psk
            sct_last = None
            pv_last = None
            for n in range(NST):
                ak, aslot, aw = dAr.acquire()
                datok = sch.step("scalar", lambda e, d=dslot, a=aslot, c=c, n=n:
                                 e.activation(out=a, in_=d, func=AF.Exp,
                                              scale=ares[:, c * NST + n:
                                                         c * NST + n + 1]),
                                 waits=[dtok, t_a2] + aw,
                                 tag=f"dA{k}.{h}.{c}.{n}")
                bk2, bslot2, bw2 = dBr.acquire()
                dbtok = sch.step("vector", lambda e, u=uslot, b=bslot2, n=n:
                                 e.tensor_tensor(out=b, in0=u, in1=bbt[n],
                                                 op=ALU.mult),
                                 waits=[utok, bc_ready[n]] + bw2,
                                 tag=f"dB{k}.{h}.{c}.{n}")
                sk2, sslot2, sw2 = srng.acquire()
                sctok = sch.step("vector", lambda e, a=aslot, b=bslot2, s=sslot2,
                                 c=c, n=n: e.tensor_tensor_scan(
                    out=s, data0=a, data1=b,
                    initial=state[:, c * NST + n:c * NST + n + 1],
                    op0=ALU.mult, op1=ALU.add),
                    waits=[datok, dbtok, t_z1] + sw2, tag=f"sc{k}.{h}.{c}.{n}")
                dAr.release(ak, sctok)
                dBr.release(bk2, sctok)
                sct_last = sctok
                pk, pslot, pw = pr.acquire()
                ptok = sch.step("vector", lambda e, s=sslot2, p=pslot, n=n:
                                e.tensor_tensor(out=p, in0=s, in1=cbt[n],
                                                op=ALU.mult),
                                waits=[sctok, bc_ready[NST + n]] + pw,
                                tag=f"p{k}.{h}.{c}.{n}")
                sttok = sch.step("vector", lambda e, s=sslot2, c=c, n=n:
                                 e.tensor_copy(
                    out=state[:, c * NST + n:c * NST + n + 1],
                    in_=s[:, HF - 1:HF]), waits=[sctok],
                    tag=f"st{k}.{h}.{c}.{n}", small=True, strict=True)
                srng.release(sk2, sttok)
                pv_last = ptok
                gtok_prev = sch.step("vector", lambda e, y=yslot, p=pslot:
                                     e.tensor_tensor(out=y, in0=y, in1=p,
                                                     op=ALU.add),
                                     waits=[ptok, gtok_prev],
                                     tag=f"yA{k}.{h}.{c}.{n}")
                pr.release(pk, gtok_prev)
            ymm = gtok_prev
            uten.release(uk, pv_last)
            delta.release(dk, sct_last)
            gk, gslot, gtok = ctx["gate_tok"][c]
            ygw = [ymm, gtok]
            if h == 0:
                dst = ygstage[:, c * HF:(c + 1) * HF]
                if last["ycommit"][c]:
                    ygw.append(last["ycommit"][c])
            else:
                dst = ygt[:, c * T + HF:(c + 1) * T]
                if last["ygt_kc"][c]:
                    ygw.append(last["ygt_kc"][c])
            ygtok = sch.step("vector", lambda e, gsl=gslot, y=yslot, d=dst:
                             e.tensor_tensor(out=d, in0=y, in1=gsl,
                                             op=ALU.mult),
                             waits=ygw, tag=f"yg{k}.{h}.{c}")
            gate.release(gk, ygtok)
            ybuf.release(yk, ygtok)
            if h == 0:
                stage_tok[c] = ygtok
            else:
                ctx["ygt_tok"][c] = ygtok
            if c + 2 < CD:
                emit_dt(k, h, c + 2)
            if c % 2 == 1 and (c // 2) + 1 < 8:
                emit_gate_og(k, h, (c // 2) + 1)
            budget = 21.0
            while queue and budget > 0:
                cost, it = queue.pop(0)
                it()
                budget -= cost
        while queue:
            queue.pop(0)[1]()
        if h == 0:
            for c in range(CD):
                cw = [stage_tok[c]]
                if last["ygt_kc"][c]:
                    cw.append(last["ygt_kc"][c])
                ctok = sch.step(
                    "scalar", lambda e, c=c: e.copy(
                        out=ygt[:, c * T:c * T + HF],
                        in_=ygstage[:, c * HF:(c + 1) * HF]),
                    waits=cw, tag=f"ycommit{k}.{c}")
                ctx["ygt_tok"][c] = ctok
                last["ycommit"][c] = ctok
        last["bcb_reader"] = pv_last
        last["rr2_reader"] = pv_last

    def make_ffn(k):
        fc = {"h1in_tok": [], "h2in_tok": [], "h3_tok": [],
              "xn_rd": [], "ffnB_rd": []}

        def w1w2_oc(oc):
            def go():
                k1, s1_, t1 = mats["w1"].load(0, KD, oc * 128, 128)
                k2, s2_, t2 = mats["w2"].load(0, KD, oc * 128, 128)
                bank0, bank1 = (0, 1) if oc % 2 == 0 else (2, 3)
                w1w = [t1]
                if last["pf_free"][bank0]:
                    w1w.append(last["pf_free"][bank0])
                w2w = [t2]
                if last["pf_free"][bank1]:
                    w2w.append(last["pf_free"][bank1])
                def wgrp(e, s, b):
                    mm = None
                    for kb in range(KD):
                        mm = e.matmul(out=pf[b][:, 0:T],
                                      lhsT=s[:, kb * 128:(kb + 1) * 128],
                                      rhs=xnbf[:, kb * T:(kb + 1) * T],
                                      start=(kb == 0), stop=(kb == KD - 1),
                                      skip_group_check=True)
                    return mm
                m1tok = sch.step("tensor", lambda e, s=s1_, b=bank0: wgrp(e, s, b),
                                 waits=w1w, tag=f"w1mm{k}.{oc}")
                m2tok = sch.step("tensor", lambda e, s=s2_, b=bank1: wgrp(e, s, b),
                                 waits=w2w, tag=f"w2mm{k}.{oc}")
                fc["xn_rd"].append(m2tok)
                wring.release(k1, m1tok)
                wring.release(k2, m2tok)
                sk, ss, sw = scr1.acquire()
                s1tok = sch.step("scalar", lambda e, s=ss, b=bank0, oc=oc:
                                 e.activation(out=s, in_=pf[b][:, 0:T],
                                              func=AF.Silu,
                                              scale=s1c[:, oc:oc + 1]),
                                 waits=[m1tok] + sw, tag=f"m1.{k}.{oc}")
                wA = [m2tok, s1tok]
                if last["ffnA_reader"]:
                    wA.append(last["ffnA_reader"])
                htok = sch.step("vector", lambda e, s=ss, b=bank1, oc=oc:
                                e.scalar_tensor_tensor(
                                    out=ffnA[:, oc * T:(oc + 1) * T],
                                    in0=pf[b][:, 0:T], scalar=s2c[:, oc:oc + 1],
                                    in1=s, op0=ALU.mult, op1=ALU.mult),
                                waits=wA, tag=f"h1in{k}.{oc}")
                scr1.release(sk, htok)
                last["pf_free"][bank0] = s1tok
                last["pf_free"][bank1] = htok
                fc["h1in_tok"].append(htok)
            return go

        def hmat_pair(mat, src, dst, scales, oc0, out_list, dep_list, rd_list,
                      wrguard, banks, free_key):
            def go():
                mtoks = []
                for i in range(2):
                    oc = oc0 + i
                    bank = (oc0 // 2 + i) % len(banks) if False else \
                        (oc % len(banks))
                    ka, sa, ta = mats[mat].load(0, 16, oc * 128, 128)
                    w0 = [ta] + list(fc[dep_list])
                    if last[free_key][bank]:
                        w0.append(last[free_key][bank])

                    def hgrp(e, s, b, koff, first, last_):
                        mm = None
                        for kk in range(16):
                            mm = e.matmul(out=banks[b][:, 0:T],
                                          lhsT=s[:, kk * 128:(kk + 1) * 128],
                                          rhs=src[:, (koff + kk) * T:
                                                   (koff + kk + 1) * T],
                                          start=(first and kk == 0),
                                          stop=(last_ and kk == 15),
                                          skip_group_check=True)
                        return mm
                    mtok = sch.step("tensor", lambda e, s=sa, b=bank:
                                    hgrp(e, s, b, 0, True, False),
                                    waits=w0, tag=f"{mat}a{k}.{oc}")
                    wring.release(ka, mtok)
                    kb2, sb2, tb = mats[mat].load(16, 16, oc * 128, 128)
                    mtok = sch.step("tensor", lambda e, s=sb2, b=bank:
                                    hgrp(e, s, b, 16, False, True),
                                    waits=[tb], tag=f"{mat}b{k}.{oc}")
                    wring.release(kb2, mtok)
                    mtoks.append(mtok)
                    if rd_list is not None:
                        fc[rd_list].append(mtok)
                        last["ffnB_reader"] = mtok
                for i in range(2):
                    oc = oc0 + i
                    bank = oc % len(banks)
                    wE = [mtoks[i]]
                    if wrguard is not None and last[wrguard]:
                        wE.append(last[wrguard])
                    etok = sch.step("scalar", lambda e, b=bank, oc=oc:
                                    e.activation(
                        out=dst[:, oc * T:(oc + 1) * T], in_=banks[b][:, 0:T],
                        func=AF.Gelu, scale=scales[:, oc:oc + 1]),
                        waits=wE, tag=f"{mat}e{k}.{oc}")
                    last[free_key][bank] = etok
                    fc[out_list].append(etok)
            return go

        w12h1 = [w1w2_oc(oc) for oc in range(KH)]
        w12h1 += [hmat_pair("h1", ffnA, ffnB, h1sc, oc0, "h2in_tok", "h1in_tok",
                            None, "ffnB_reader", pf, "pf_free")
                  for oc0 in range(0, KH, 2)]
        h2s = [hmat_pair("h2", ffnB, ffnA, h2sc, oc0, "h3_tok", "h2in_tok",
                         "ffnB_rd", "ffnA_reader", py, "py_free")
               for oc0 in range(0, KH, 2)]
        return fc, w12h1, h2s

    def make_tail(k, tctx):
        par = k % 2
        fc = tctx["fc"]
        items = []

        def oh_items(oh):
            res = []
            mms = []

            def go_out(kcg):
                def go():
                    kslot, slot, wtok = mats["out"].load(kcg * 4, 4, oh * 512, 512)
                    w0 = [wtok]
                    for i in range(4):
                        w0 += list(tctx["ygt_tok"][kcg * 4 + i])
                    if kcg == 0:
                        for ts in range(4):
                            if last["pf_free"][ts]:
                                w0.append(last["pf_free"][ts])

                    def ogrp(e, s=slot, kcg=kcg):
                        mm = None
                        for kc4 in range(4):
                            kc = kcg * 4 + kc4
                            for ts in range(4):
                                mm = e.matmul(
                                    out=pf[ts][:, 0:512],
                                    lhsT=ygt[:, kc * T + ts * 128:
                                             kc * T + (ts + 1) * 128],
                                    rhs=s[:, kc4 * 512:(kc4 + 1) * 512],
                                    start=(kc == 0), stop=False,
                                    skip_group_check=True)
                        return mm
                    mtok = sch.step("tensor", ogrp, waits=w0,
                                    tag=f"op{k}.{oh}.{kcg}")
                    if oh == 1:
                        for i in range(4):
                            last["ygt_kc"][kcg * 4 + i] = mtok
                    wring.release(kslot, mtok)
                    mms.append(mtok)
                return go

            def go_w3(kcg):
                def go():
                    kslot, slot, wtok = mats["w3"].load(kcg * 4, 4, oh * 512, 512)
                    w0 = [wtok] + [fc["h3_tok"][kcg * 4 + i] for i in range(4)]

                    def wgrp3(e, s=slot, kcg=kcg):
                        mm = None
                        for kc4 in range(4):
                            kc = kcg * 4 + kc4
                            for ts in range(4):
                                mm = e.matmul(
                                    out=pf[ts][:, 0:512],
                                    lhsT=ffnA[:, kc * T + ts * 128:
                                              kc * T + (ts + 1) * 128],
                                    rhs=s[:, kc4 * 512:(kc4 + 1) * 512],
                                    start=False, stop=(kcg == 7 and kc4 == 3),
                                    skip_group_check=True)
                        return mm
                    mtok = sch.step("tensor", wgrp3, waits=w0,
                                    tag=f"w3.{k}.{oh}.{kcg}")
                    if oh == 1:
                        last["ffnA_reader"] = mtok
                    wring.release(kslot, mtok)
                    mms.append(mtok)
                return go

            def go_evac():
                def go():
                    for ts in range(4):
                        row0 = k * T + ts * 128
                        ek, eslot, ew = exr.acquire()
                        xtok = sch.step("scalar", lambda e, d=eslot,
                                        a=x_hbm[row0:row0 + 128,
                                                oh * 512:(oh + 1) * 512]:
                                        e.dma_start(out=d, in_=a), waits=ew,
                                        sem=f"EX{ek}", inc=16,
                                        tag=f"ex{k}.{ts}.{oh}")
                        ok_, oslot, ow = outr.acquire()
                        v1 = sch.step("vector", lambda e, x2=eslot, o=oslot,
                                      ts=ts: e.scalar_tensor_tensor(
                            out=o, in0=x2,
                            scalar=stats[:, par * 16 + ts:par * 16 + ts + 1],
                            in1=pf[ts][:, 0:512], op0=ALU.mult, op1=ALU.add),
                            waits=[xtok] + list(mms) + ow,
                            tag=f"ev1.{k}.{ts}.{oh}")
                        exr.release(ek, v1)
                        last["pf_free"][ts] = v1
                        v2 = sch.step("scalar", lambda e, o=oslot, ts=ts:
                                      e.activation(
                            out=o, in_=o, func=AF.Identity,
                            bias=stats[:, par * 16 + 4 + ts:
                                       par * 16 + 4 + ts + 1]),
                            waits=[v1], tag=f"ev2.{k}.{ts}.{oh}")
                        otok = dma(out_hbm[row0:row0 + 128,
                                           oh * 512:(oh + 1) * 512],
                                   oslot, waits=[v2], sem=f"DO{ok_}",
                                   tag=f"o{k}.{ts}.{oh}")
                        outr.release(ok_, otok)
                        tctx["out_toks"].append(otok)
                    mms.clear()
                return go

            res = [go_out(g) for g in range(4)] + [go_w3(g) for g in range(8)]
            res.append(go_evac())
            return res

        items += oh_items(0)
        items += oh_items(1)
        return items

    def weave(h2p, tailq):
        h2p = [(14.0, f) for f in h2p]
        tailq = [(5.0, f) for f in tailq]
        q = []
        hi = 0
        for j in range(4):
            q += [h2p[hi], h2p[hi + 1], tailq[j]]
            hi += 2
        for j in range(8):
            if hi < len(h2p):
                q += [h2p[hi], h2p[hi + 1]]
                hi += 2
            q.append(tailq[4 + j])
        q += h2p[hi:]
        q += tailq[12:]
        return q

    prev_tctx = None
    pending_h2 = []
    for k in range(nchunks):
        ctx = {"ygt_tok": {}}
        emit_head(k, 0)
        emit_mix(k, 0)
        pump = []
        if prev_tctx is not None:
            pump = weave(pending_h2, make_tail(k - 1, prev_tctx))
        elif pending_h2:
            pump = [(14.0, f) for f in pending_h2]
        pending_h2 = []
        emit_scan(k, 0, pump_items=pump)
        if prev_tctx is not None:
            last["epi"] = prev_tctx["out_toks"][-1]
        commitsA = dict(ctx["ygt_tok"])
        ctx = {"ygt_tok": {}}
        emit_head(k, 1)
        emit_mix(k, 1)
        fc, w12h1, h2items = make_ffn(k)
        costed = [(3.5, f) for f in w12h1[:KH]] + [(14.0, f) for f in w12h1[KH:]]
        emit_scan(k, 1, pump_items=costed)
        pending_h2 = h2items
        last["xn_reader"] = fc["xn_rd"][-1]
        tctx = {"ygt_tok": {c: [commitsA[c], ctx["ygt_tok"][c]]
                            for c in range(CD)},
                "fc": fc, "out_toks": []}
        prev_tctx = tctx

    for it in pending_h2:
        it()
    for it in make_tail(nchunks - 1, prev_tctx):
        it()

    tapspec = os.environ.get("MKTAPS", "")
    if tapspec:
        allt = dict(xnbf=xnbf, xssm=xssm, ygt=ygt, ffnA=ffnA, ffnB=ffnB,
                    state=state, stats=stats, rr2b=rr2b, ares=ares,
                    deltar=deltar, bcsb=bcsb, w3sb=w3sb, xpbf=xpbf, dtbf=dtbf,
                    convw=convw, normw=normw, tails=tails)
        for n_ in range(NST):
            allt[f"bb{n_}"] = bbt[n_]
            allt[f"cb{n_}"] = cbt[n_]
        eng_fin = [(s, sch.total[s]) for s in ("T", "V", "S") if s in sch.total]
        for tn in tapspec.split(","):
            ap = allt[tn]
            dt_ = ap.tensor.dtype
            tout = nc.dram_tensor("tap_" + tn, list(ap.shape), dt_,
                                  kind="ExternalOutput").ap()
            sch.step("sync", lambda e, a=ap, o=tout: e.dma_start(out=o, in_=a),
                     waits=eng_fin, sem="DO0", tag=f"tap{tn}")
    eng_sems = {"T", "V", "S", "G"}
    fin_waits = [(s, sch.total[s]) for s in sorted(sch.total)
                 if s not in eng_sems]
    sch.step("sync", lambda e: e.nop(), waits=fin_waits, sem="DW", tag="fin")

    with nc.Block() as block:
        sch.emit(nc, block)
    return nc


def _prep(inputs):
    f = np.float32
    x = np.ascontiguousarray(np.asarray(inputs["L0_attn_x"], dtype=f))
    w_in_t = np.ascontiguousarray(np.asarray(inputs["L0_attn_in_proj_w"], f).T)
    w_out_t = np.ascontiguousarray(np.asarray(inputs["L0_attn_out_proj_w"], f).T)
    xproj = np.asarray(inputs["L0_attn_x_proj_w"], f)
    w_xp_fm = np.zeros((128, CD * 96), dtype=f)
    for c in range(CD):
        w_xp_fm[:, c * 96:(c + 1) * 96] = xproj[:, c * 128:(c + 1) * 128].T
    w_dt_t = np.ascontiguousarray(np.asarray(inputs["L0_attn_dt_proj_w"], f).T)
    w1_t = np.ascontiguousarray(np.asarray(inputs["L0_ffn_w1_w"], f).T)
    w2_t = np.ascontiguousarray(np.asarray(inputs["L0_ffn_w2_w"], f).T)
    h1_t = np.ascontiguousarray(np.asarray(inputs["L0_ffn_h1_w"], f).T)
    h2_t = np.ascontiguousarray(np.asarray(inputs["L0_ffn_h2_w"], f).T)
    w3_t = np.ascontiguousarray(np.asarray(inputs["L0_ffn_w3_w"], f).T)

    def fmcols(v, ncol):
        return np.ascontiguousarray(np.asarray(v, f).reshape(ncol, 128).T)

    conv = np.asarray(inputs["L0_attn_conv_w"], f).reshape(DI, 4)
    conv_fm = np.zeros((128, CD * 4), dtype=f)
    for c in range(CD):
        conv_fm[:, c * 4:(c + 1) * 4] = conv[c * 128:(c + 1) * 128, :]
    alog = np.asarray(inputs["L0_attn_A_log"], f)
    alog_fm = np.zeros((128, CD * NST), dtype=f)
    for c in range(CD):
        alog_fm[:, c * NST:(c + 1) * NST] = alog[c * 128:(c + 1) * 128, :]

    common = dict(
        w_in_t=w_in_t, w_out_t=w_out_t, w_xp_fm=w_xp_fm, w_dt_t=w_dt_t,
        w1_t=w1_t, w2_t=w2_t, h1_t=h1_t, h2_t=h2_t, w3_t=w3_t,
        conv_fm=conv_fm,
        convb_fm=fmcols(inputs["L0_attn_conv_b"], CD),
        dtb_fm=fmcols(inputs["L0_attn_dt_proj_b"], CD),
        dsk_fm=fmcols(inputs["L0_attn_D_skip"], CD),
        alog_fm=alog_fm,
        norm_fm=fmcols(inputs["L0_attn_norm_w"], KD),
        s1_fm=fmcols(inputs["L0_ffn_w1_s"], KH),
        s2_fm=fmcols(inputs["L0_ffn_w2_s"], KH),
        h1s_fm=fmcols(inputs["L0_ffn_h1_s"], KH),
        h2s_fm=fmcols(inputs["L0_ffn_h2_s"], KH),
        w3s_row=np.ascontiguousarray(np.asarray(inputs["L0_ffn_w3_s"], f)
                                     .reshape(1, D)),
        ident_in=np.eye(128, dtype=f),
        ones_in=np.ones((1, 128), dtype=f),
    )
    in_maps = []
    for b in range(8):
        m = dict(common)
        m["x"] = np.ascontiguousarray(x[b])
        in_maps.append(m)
    return in_maps


def kernel(**inputs):
    in_maps = _prep(inputs)
    nc = build()
    res = run_bass_kernel_spmd(nc, in_maps, core_ids=list(range(8)))
    out = np.stack([np.asarray(r["out"], dtype=np.float32)
                    for r in res.results], axis=0)
    return out
